# version 19
# baseline (speedup 1.0000x reference)
import sys
sys.path.insert(0, '/opt/trn_rl_repo')
import numpy as np
import ml_dtypes
from contextlib import ExitStack

import concourse.bass as bass
import concourse.mybir as mybir
from concourse.alu_op_type import AluOpType as OP
from concourse.bass_utils import run_bass_kernel_spmd

F32 = mybir.dt.float32
BF16 = mybir.dt.bfloat16
AF = mybir.ActivationFunctionType
AX = mybir.AxisListType
bf16 = ml_dtypes.bfloat16

MAGIC = 12582912.0
NEGM = -3.0e38
EPS = 1e-5
DH = 64
WS = 128
NH = 8
D = 1024
DI = 4096

ENGS = ("sync", "vector", "scalar", "tensor", "gpsimd")


def _base_tables():
    f = np.arange(0, DH, 2, dtype=np.float64) / DH
    inv_freq = 1.0 / 10000.0 ** f
    sbase = (np.arange(0, DH, 2, dtype=np.float64) + 0.4 * DH) / (1.4 * DH)
    return inv_freq, sbase


def _rot_tables():
    inv_freq, sbase = _base_tables()
    i = np.arange(WS, dtype=np.float64)
    out = {}
    for c in range(3):
        ang = np.outer(256.0 + i - 128.0 * (c - 1), inv_freq)
        expo = (64.0 + i + 128.0 * (1 - c)) / (WS // 2)
        scale = sbase[None, :] ** expo[:, None]
        cc = np.cos(ang) * scale * 0.125
        ss = np.sin(ang) * scale * 0.125
        out[f"rqc{c}"] = np.concatenate([cc, cc], 1).T
        out[f"rqs{c}"] = np.concatenate([-ss, ss], 1).T
    ang = np.outer(128.0 + i, inv_freq)
    expo = -(i - 64.0) / (WS // 2)
    scale = sbase[None, :] ** expo[:, None]
    cc = np.cos(ang) * scale
    ss = np.sin(ang) * scale
    out["rkc"] = np.concatenate([cc, cc], 1).T
    out["rks"] = np.concatenate([-ss, ss], 1).T
    res = {}
    for c in range(3):
        cc = np.tile(out[f"rqc{c}"], (1, NH)).astype(bf16)
        ss = np.tile(out[f"rqs{c}"], (1, NH)).astype(bf16)
        res[f"rq{c}"] = np.concatenate([cc, ss], axis=0)
    res["rkc"] = np.tile(out["rkc"], (1, NH)).astype(bf16)
    kss = np.tile(out["rks"], (1, NH)).astype(bf16)
    res["rks"] = np.concatenate([np.zeros_like(kss), kss], axis=0)
    return res


def host_consts():
    c = _rot_tables()
    i = np.arange(WS)[:, None]
    jc = np.arange(WS)[None, :]
    c["mask0"] = np.where(jc >= i, 0.0, NEGM).astype(bf16)
    c["mask2"] = np.where(jc <= i, 0.0, NEGM).astype(bf16)
    c["identb"] = np.eye(128, dtype=bf16)
    c["onescol"] = np.ones((128, 1), dtype=np.float32)
    c["onesrow"] = np.ones((1, 128), dtype=np.float32)
    c["onesrowb"] = np.ones((1, 128), dtype=bf16)
    return c


class Sem:
    __slots__ = ("h", "n", "name")

    def __init__(self, h, name):
        self.h = h
        self.n = 0
        self.name = name


class Prog:
    def __init__(self, nc, stack):
        self.nc = nc
        self.stack = stack
        self.q = {e: [] for e in ENGS}
        self.wm = {e: {} for e in ENGS}
        self.free = {}
        self.all_sems = []
        self.chain = {}
        self.wsig = {e: {} for e in ENGS}
        self.rsig = {e: {} for e in ENGS}

    def sem(self, name):
        s = Sem(self.stack.enter_context(self.nc.semaphore(name)), name)
        self.all_sems.append(s)
        return s

    def emit(self, eng, fn, waits=(), incs=(), reads=(), writes=()):
        wl_raw = list(waits)
        for r in reads:
            sig = self.wsig[eng].get(r)
            if sig:
                wl_raw.append(sig)
        for w in writes:
            for d in (self.wsig[eng], self.rsig[eng]):
                sig = d.get(w)
                if sig:
                    wl_raw.append(sig)
        wl = []
        for s, v in wl_raw:
            if v <= 0:
                continue
            if self.wm[eng].get(s.name, -1) >= v:
                continue
            self.wm[eng][s.name] = v
            wl.append((s.h, v))
        incs = list(incs)
        if not incs and (reads or writes):
            if eng not in self.chain:
                self.chain[eng] = self.sem(f"chain_{eng}")
            incs = [(self.chain[eng], 1)]
        for s, k in incs:
            s.n += k
        inc_l = [(s.h, k) for s, k in incs]
        mysig = (incs[0][0], incs[0][0].n) if incs else None
        if mysig:
            for w in writes:
                self.wsig[eng][w] = mysig
            for r in reads:
                self.rsig[eng][r] = mysig
        self.q[eng].append((False, wl, fn, inc_l))

    def wait_free(self, key):
        return self.free.pop(key, [])

    def set_free(self, key, *sv):
        self.free[key] = [(s, v) for s, v in sv]

    def replay(self, blk):
        def mk(steps):
            def body(e):
                for need_drain, wl, fn, incl in steps:
                    if need_drain:
                        e.drain()
                    for h, v in wl:
                        e.wait_ge(h, v)
                    inst = fn(e)
                    for h, k in incl:
                        inst.then_inc(h, k)
            return body
        for eng in ENGS:
            getattr(blk, eng)(mk(self.q[eng]))

    def barrier(self, name):
        bar = self.sem(name)
        for eng in ENGS:
            self.emit(eng, lambda e: e.nop(), incs=[(bar, 1)])
        for eng in ENGS:
            self.emit(eng, lambda e: e.nop(), waits=[(bar, 5)])


def build_nc(n_seq, wps):
    NT = n_seq * wps
    TOK = NT * 128
    nc = bass.Bass("TRN2", target_bir_lowering=False)
    consts = host_consts()

    x_in = nc.dram_tensor("x", [TOK, D], F32, kind="ExternalInput").ap()
    wqkvT = nc.dram_tensor("wqkvT", [D, D], F32, kind="ExternalInput").ap()
    woutT = nc.dram_tensor("woutT", [512, D], F32, kind="ExternalInput").ap()
    wgT = nc.dram_tensor("wgT", [D, DI], F32, kind="ExternalInput").ap()
    wuT = nc.dram_tensor("wuT", [D, DI], F32, kind="ExternalInput").ap()
    wdT = nc.dram_tensor("wdT", [DI, D], F32, kind="ExternalInput").ap()
    g_pc = nc.dram_tensor("g_pc", [128, 8], F32, kind="ExternalInput").ap()
    b_pc = nc.dram_tensor("b_pc", [128, 8], F32, kind="ExternalInput").ap()
    cap = {}
    for k, v in consts.items():
        dt = BF16 if v.dtype == bf16 else F32
        cap[k] = nc.dram_tensor(k, list(v.shape), dt, kind="ExternalInput").ap()
    y_out = nc.dram_tensor("y", [TOK, D], F32, kind="ExternalOutput").ap()

    xn_buf = nc.dram_tensor("xn_buf", [TOK, D], F32, kind="Internal").ap()
    pa_buf = nc.dram_tensor("pa_buf", [TOK, D], F32, kind="Internal").ap()
    hq_buf = nc.dram_tensor("hq_buf", [TOK, DI], BF16, kind="Internal").ap()
    wqg_buf = nc.dram_tensor("wqg_buf", [D, DI], BF16, kind="Internal").ap()
    wqu_buf = nc.dram_tensor("wqu_buf", [D, DI], BF16, kind="Internal").ap()
    wqd_buf = nc.dram_tensor("wqd_buf", [DI, D], BF16, kind="Internal").ap()

    st = ExitStack()
    with st:
        P = Prog(nc, st)
        E = P.emit
        S = P.sem
        sb = lambda name, shape, dt: st.enter_context(nc.sbuf_tensor(name, shape, dt))
        ps = lambda name, shape, dt: st.enter_context(nc.psum_tensor(name, shape, dt))

        ident = sb("ident", [128, 128], BF16)
        onescol = sb("onescol_s", [128, 1], F32)
        onesrow = sb("onesrow_s", [1, 128], F32)
        onesrowb = sb("onesrowb_s", [1, 128], BF16)
        invwbc = sb("invwbc", [128, 4], F32)
        combs = sb("combs", [128, NT], F32)
        beff = sb("beff", [1, D], BF16)
        gsbuf = sb("gsbuf", [128, 8], F32)
        bsbuf = sb("bsbuf", [128, 8], F32)

        ast = ExitStack()
        sb = lambda name, shape, dt: ast.enter_context(nc.sbuf_tensor(name, shape, dt))
        ps = lambda name, shape, dt: ast.enter_context(nc.psum_tensor(name, shape, dt))
        tabs = {k: sb("t_" + k, list(consts[k].shape),
                      BF16 if consts[k].dtype == bf16 else F32)
                for k in ("rq0", "rq1", "rq2", "rkc", "rks",
                          "mask0", "mask2")}
        xb = [sb(f"xb{i}", [128, D], F32) for i in range(6)]
        xn_sb = [sb(f"xn{i}", [128, D], F32) for i in range(2)]
        xn2b = [sb(f"xn2b{i}", [128, D], BF16) for i in range(2)]
        xn2T = [sb(f"xn2T{i}", [128, D], BF16) for i in range(2)]
        qk_sb = [sb(f"qksb{i}", [128, 512], BF16) for i in range(2)]
        qkraw = [sb(f"qkraw{i}", [128, 1024], BF16) for i in range(2)]
        krot = [sb(f"krot{i}", [128, 1024], BF16) for i in range(6)]
        vaug = [sb(f"vaug{i}", [128, 8 * 65], BF16) for i in range(6)]
        qrot = [[sb(f"qrot{p}_{c}", [128, 1024], BF16) for c in range(3)]
                for p in range(4)]
        rtv = [sb(f"rtv{i}", [64, 1024], BF16) for i in range(2)]
        exp_sb = [[sb(f"exp{p}_{i}", [128, 384], BF16) for i in range(8)]
                  for p in range(2)]
        expT = [sb(f"expT{i}", [128, 384], BF16) for i in range(2)]
        attn_tm = [sb(f"attntm{i}", [128, 512], BF16) for i in range(2)]
        attnT = [sb(f"attnT{i}", [128, 512], BF16) for i in range(2)]
        pa_sb = [sb(f"pasb{i}", [128, D], F32) for i in range(2)]
        stats = [sb(f"stats{i}", [128, 32], F32) for i in range(2)]
        nmax = [sb(f"nmax{i}", [128, 1], F32) for i in range(3)]
        rcp = sb("rcp", [128, 1], F32)
        rcps = [sb(f"rcpb{i}", [128, 1], F32) for i in range(4)]
        wqkv_sb = [sb(f"wqkv{i}", [128, D], BF16) for i in range(8)]
        wout_sb = [sb(f"wout{i}", [128, D], BF16) for i in range(4)]
        wstage = [sb(f"wstage{i}", [128, DI], F32) for i in range(2)]
        wqstage = [sb(f"wqstage{i}", [128, DI], BF16) for i in range(1)]
        wsums = sb("wsums", [128, 32], F32)
        wtot = sb("wtot", [1, 4], F32)

        pT = [st.enter_context(nc.psum_tensor(f"pT{i}", [128, 512], BF16))
              for i in range(2)]
        pQO = ps("pQO", [128, 1024], F32)
        pS = [ps(f"pS{i}", [128, 384], F32) for i in range(3)]
        pA = [ps(f"pA{i}", [128, 260], F32) for i in range(1)]

        LdC = S("LdC")
        LdX = [S(f"LdX{i}") for i in range(6)]
        StXn = [S(f"StXn{i}") for i in range(2)]
        StPa = [S(f"StPa{i}") for i in range(2)]
        Vln = S("Vln"); Ssq = S("Ssq"); Vst2 = S("Vst2")
        Sxn = S("Sxn"); Sxn2 = S("Sxn2")
        Tp = S("Tp"); Vev = S("Vev")
        Tqkv = S("Tqkv"); Vqk = S("Vqk"); Vrot = S("Vrot"); Grot = S("Grot")
        Tsim = S("Tsim"); Vmax = S("Vmax"); Sexp = S("Sexp")
        Tav = S("Tav"); Vav = S("Vav")
        Tout = S("Tout"); Vpa = S("Vpa"); Vrc = S("Vrc"); Sac = S("Sac")
        LdW = [S(f"LdW{i}") for i in range(2)]
        Vw1 = S("Vw1"); Tw = S("Tw"); Vw2 = S("Vw2"); Gw = S("Gw")
        StWq = [S(f"StWq{i}") for i in range(2)]
        LdQP = [S(f"LdQP{i}") for i in range(2)]
        TbeC = S("TbeC")
        Vqp = S("Vqp")

        def tp_block(in_tiles, out_slices, bank_ids, fillsem, evacsem,
                     extra_first_waits=(), evac_dtkey=None, part64=False,
                     evac_eng="scalar"):
            nb = len(bank_ids)
            for i, ap_in in enumerate(in_tiles):
                b = bank_ids[i // 4]
                wl = list(extra_first_waits) if i == 0 else []
                if i % 4 == 0:
                    wl += P.wait_free(("pT", b))
                dst = pT[b][0:64, (i % 4) * 128:((i % 4) + 1) * 128] if part64 \
                    else pT[b][:, (i % 4) * 128:((i % 4) + 1) * 128]
                E("tensor", (lambda e, dst=dst, ap_in=ap_in: e.transpose(
                    out=dst, in_=ap_in, identity=ident[:])),
                  waits=wl, incs=([(Tp, 1)] if i % 4 == 3 else []))
                if i % 4 == 3:
                    tp_here = Tp.n
                    out_ap, out_name = out_slices[i // 4]
                    src = pT[b][0:64, :] if part64 else pT[b][:]
                    wl2 = [(Tp, tp_here)] + P.wait_free(("ev", out_name))
                    if evac_eng == "scalar":
                        E("scalar", (lambda e, out_ap=out_ap, src=src: e.copy(
                            out=out_ap, in_=src)),
                          waits=wl2, incs=[(Vev, 1)], writes=[out_name])
                    else:
                        E("vector", (lambda e, out_ap=out_ap, src=src:
                                     e.tensor_copy(out=out_ap, in_=src)),
                          waits=wl2, incs=[(Vev, 1)], writes=[out_name])
                    P.set_free(("pT", b), (Vev, Vev.n))
            return Vev.n

        for k in tabs:
            E("sync", (lambda e, d=tabs[k], s=cap[k]: e.dma_start(out=d[:], in_=s)),
              incs=[(LdC, 16)])
        E("sync", lambda e: e.dma_start(out=ident[:], in_=cap["identb"]), incs=[(LdC, 16)])
        E("sync", lambda e: e.dma_start(out=onescol[:], in_=cap["onescol"]), incs=[(LdC, 16)])
        E("sync", lambda e: e.dma_start(out=onesrow[:], in_=cap["onesrow"]), incs=[(LdC, 16)])
        E("sync", lambda e: e.dma_start(out=onesrowb[:], in_=cap["onesrowb"]), incs=[(LdC, 16)])
        E("sync", lambda e: e.dma_start(out=gsbuf[:], in_=g_pc), incs=[(LdC, 16)])
        E("sync", lambda e: e.dma_start(out=bsbuf[:], in_=b_pc), incs=[(LdC, 16)])
        ldc_all = LdC.n
        Vms = S("Vms")
        for s0 in range(6):
            E("vector", (lambda e, v=vaug[s0]: e.memset(
                v[:].rearrange("p (h c) -> p h c", c=65)[:, :, 64:65], 1.0)),
              writes=[f"vaug{s0}"], incs=[(Vms, 1)])

        x_loaded = {}
        for t0 in range(min(2, NT)):
            E("sync", (lambda e, t0=t0: e.dma_start(
                out=xb[t0 % 6][:], in_=x_in[t0 * 128:(t0 + 1) * 128, :])),
              incs=[(LdX[t0 % 6], 16)])
            x_loaded[t0] = (LdX[t0 % 6], LdX[t0 % 6].n)

        for c in range(8):
            stg = wstage[c % 2]
            qp = LdQP[c % 2]
            E("sync", (lambda e, d=stg, s=wqkvT[c * 128:(c + 1) * 128, :]:
                       e.dma_start(out=d[:, 0:D], in_=s)),
              waits=[(Vqp, max(0, Vqp.n - 1)), (TbeC, max(0, TbeC.n - 1))],
              incs=[(qp, 16)])
            qp_t = qp.n
            for nb in range(2):
                E("tensor", (lambda e, stg=stg, c=c, nb=nb: e.matmul(
                    out=pQO[0:1, nb * 512:(nb + 1) * 512],
                    lhsT=bsbuf[:, c:c + 1], rhs=stg[:, nb * 512:(nb + 1) * 512],
                    start=(c == 0), stop=(c == 7))),
                  waits=([(qp, qp_t), (LdC, ldc_all)] if nb == 0 else []),
                  incs=([(TbeC, 1)] if nb == 1 else []))
            E("vector", (lambda e, stg=stg, c=c: e.tensor_scalar(
                out=wqkv_sb[c][:], in0=stg[:, 0:D], scalar1=gsbuf[:, c:c + 1],
                scalar2=None, op0=OP.mult)),
              waits=[(qp, qp_t), (TbeC, TbeC.n), (LdC, ldc_all)],
              incs=[(Vqp, 1)], reads=[f"wstage{c % 2}"], writes=[f"wqkv{c}"])
        E("vector", lambda e: e.tensor_copy(out=beff[:], in_=pQO[0:1, :]),
          waits=[(TbeC, 8)], incs=[(Vqp, 1)], writes=["beff"])
        for c in range(4):
            stg = wstage[c % 2]
            qp = LdQP[c % 2]
            E("sync", (lambda e, d=stg, s=woutT[c * 128:(c + 1) * 128, :]:
                       e.dma_start(out=d[:, 0:D], in_=s)),
              waits=[(Vqp, max(0, Vqp.n - 1))], incs=[(qp, 16)])
            E("vector", (lambda e, stg=stg, c=c: e.tensor_copy(
                out=wout_sb[c][:], in_=stg[:, 0:D])),
              waits=[(qp, qp.n)], incs=[(Vqp, 1)],
              reads=[f"wstage{c % 2}"], writes=[f"wout{c}"])
        vqp_prep = Vqp.n

        for (wt, nch, N, qbuf, col) in ((wgT, 8, DI, wqg_buf, 0),
                                        (wuT, 8, DI, wqu_buf, 1),
                                        (wdT, 32, D, wqd_buf, 2)):
            for c in range(nch):
                stg = wstage[c % 2]
                lw = LdW[c % 2]
                E("sync", (lambda e, d=stg, s=wt[c * 128:(c + 1) * 128, :], N=N:
                           e.dma_start(out=d[:, 0:N], in_=s)),
                  waits=[(Vw1, max(0, Vw1.n - 1)), (Vqp, vqp_prep)],
                  incs=[(lw, 16)])
                E("vector", (lambda e, stg=stg, c=c, N=N: e.tensor_reduce(
                    out=wsums[:, c:c + 1], in_=stg[:, 0:N], axis=AX.X, op=OP.add,
                    apply_absolute_value=True)),
                  waits=[(lw, lw.n)], incs=[(Vw1, 1)],
                  reads=[f"wstage{c % 2}", "wsums"], writes=["wsums"])
            E("tensor", (lambda e, nch=nch: e.matmul(
                out=pS[0][0:1, 0:nch], lhsT=onescol[:], rhs=wsums[:, 0:nch],
                start=True, stop=True)),
              waits=[(Vw1, Vw1.n), (LdC, ldc_all), (Vw2, Vw2.n)],
              incs=[(Tw, 1)])
            tw1 = Tw.n
            E("vector", (lambda e, nch=nch: e.tensor_reduce(
                out=wtot[0:1, 0:1], in_=pS[0][0:1, 0:nch], axis=AX.X, op=OP.add)),
              waits=[(Tw, tw1)], reads=["wtot"], writes=["wtot"])
            E("vector", (lambda e, nch=nch, N=N: e.tensor_scalar(
                out=wtot[0:1, 1:2], in0=wtot[0:1, 0:1],
                scalar1=1.0 / (nch * 128 * N), scalar2=1e-5,
                op0=OP.mult, op1=OP.max)),
              reads=["wtot"], writes=["wtot"])
            E("vector", lambda e: e.reciprocal(out=wtot[0:1, 2:3], in_=wtot[0:1, 1:2]),
              reads=["wtot"], writes=["wtot"], incs=[(Vw2, 1)])
            vw2a = Vw2.n
            E("tensor", (lambda e: e.matmul(
                out=pA[0][:, 0:2], lhsT=onesrow[:], rhs=wtot[0:1, 1:3],
                start=True, stop=True)),
              waits=[(Vw2, vw2a)], incs=[(Tw, 1)])
            tw2 = Tw.n
            E("vector", (lambda e, col=col: e.tensor_copy(
                out=invwbc[:, col:col + 1], in_=pA[0][:, 0:1])),
              waits=[(Tw, tw2)], writes=["invwbc"], incs=[(Vw2, 1)])
            E("vector", (lambda e: e.tensor_copy(out=rcp[:], in_=pA[0][:, 1:2])),
              reads=["rcp"], writes=["rcp"], incs=[(Vw2, 1)])
            vw2b = Vw2.n
            for c in range(nch):
                stg = wstage[c % 2]
                qstg = wqstage[0]
                lw = LdW[c % 2]
                sq = StWq[c % 2]
                E("sync", (lambda e, d=stg, s=wt[c * 128:(c + 1) * 128, :], N=N:
                           e.dma_start(out=d[:, 0:N], in_=s)),
                  waits=[(Vw1, max(0, Vw1.n - 1)), (Vw2, vw2b),
                         (Gw, max(0, Gw.n - 1))],
                  incs=[(lw, 16)])
                E("vector", (lambda e, stg=stg, N=N: e.tensor_scalar(
                    out=stg[:, 0:N], in0=stg[:, 0:N], scalar1=rcp[:, 0:1],
                    scalar2=MAGIC, op0=OP.mult, op1=OP.add)),
                  waits=[(lw, lw.n), (Vw2, vw2b)], incs=[(Vw1, 1)],
                  reads=[f"wstage{c % 2}", "rcp"], writes=[f"wstage{c % 2}"])
                E("vector", (lambda e, stg=stg, N=N: e.tensor_scalar(
                    out=stg[:, 0:N], in0=stg[:, 0:N], scalar1=MAGIC, scalar2=1.0,
                    op0=OP.subtract, op1=OP.min)),
                  reads=[f"wstage{c % 2}"], writes=[f"wstage{c % 2}"])
                E("vector", (lambda e, stg=stg, qstg=qstg, N=N: e.tensor_scalar(
                    out=qstg[:, 0:N], in0=stg[:, 0:N], scalar1=-1.0, scalar2=None,
                    op0=OP.max)),
                  waits=[(StWq[0], StWq[0].n), (StWq[1], StWq[1].n)],
                  incs=[(Gw, 1)], reads=[f"wstage{c % 2}"],
                  writes=["wqstage0"])
                E("sync", (lambda e, qstg=qstg, d=qbuf[c * 128:(c + 1) * 128, :],
                           N=N: e.dma_start(out=d, in_=qstg[:, 0:N])),
                  waits=[(Gw, Gw.n)], incs=[(sq, 16)])
        stwq_all = [(StWq[0], StWq[0].n), (StWq[1], StWq[1].n)]
        P.set_free(("pA", 0), (Vw2, Vw2.n))
        prolog_ps_free = [(Vw2, Vw2.n)]


        qrot_ready = {}
        krot_ready = {}
        vaug_ready = {}

        attn_state = {}

        def emit_attn_loop1(tp):
            par = tp % 2
            par4 = tp % 4
            w = tp % wps
            cs = [c for c in range(3) if 0 <= (w + c - 1) < wps]
            lo, hi = cs[0] * 128, (cs[-1] + 1) * 128
            sexp_at = {}
            attn_state[tp] = (cs, lo, hi, sexp_at)
            for h in range(NH):
                hp = h % 3
                bank = pS[hp]
                waits = [(Sexp, max(0, Sexp.n - 2))] + prolog_ps_free
                waits += qrot_ready[par4]
                for c in cs:
                    waits += krot_ready[(tp + c - 1) % 6]
                first = True
                last_is_mask2 = (2 in cs)
                for c in cs:
                    ks = (tp + c - 1) % 6
                    sim_last = (not last_is_mask2) and (c == cs[-1])
                    E("tensor", (lambda e, c=c, h=h, bank=bank, par4=par4,
                                 ks=ks: e.matmul(
                        out=bank[:, c * 128:(c + 1) * 128],
                        lhsT=qrot[par4][c][:, h * 128:(h + 1) * 128],
                        rhs=krot[ks][:, h * 128:(h + 1) * 128],
                        start=True, stop=(c == 1))),
                      waits=(waits if first else []),
                      incs=([(Tsim, 1)] if sim_last else []))
                    first = False
                    if c == 0:
                        E("tensor", (lambda e, bank=bank: e.matmul(
                            out=bank[:, 0:128], lhsT=ident[:],
                            rhs=tabs["mask0"][:], start=False, stop=True)))
                    if c == 2:
                        E("tensor", (lambda e, bank=bank: e.matmul(
                            out=bank[:, 256:384], lhsT=ident[:],
                            rhs=tabs["mask2"][:], start=False, stop=True)),
                          incs=[(Tsim, 1)])
                tsim_h = Tsim.n
                E("vector", (lambda e, bank=bank, hp=hp, lo=lo, hi=hi:
                             e.tensor_reduce(
                    out=nmax[hp][:], in_=bank[:, lo:hi], axis=AX.X, op=OP.max,
                    negate=True)),
                  waits=[(Tsim, tsim_h), (Sexp, max(0, Sexp.n - 2))],
                  incs=[(Vmax, 1)], reads=[f"nmax{hp}"], writes=[f"nmax{hp}"])
                vmax_h = Vmax.n
                wl = [(Vmax, vmax_h)] + exp_free.pop((par, h), [])
                E("scalar", (lambda e, bank=bank, h=h, hp=hp, lo=lo, hi=hi,
                             par=par: e.activation(
                    out=exp_sb[par][h][:, lo:hi], in_=bank[:, lo:hi],
                    func=AF.Exp, bias=nmax[hp][:], scale=1.0)),
                  waits=wl, incs=[(Sexp, 1)],
                  reads=[f"exp{par}_{h}"], writes=[f"exp{par}_{h}"])
                sexp_at[h] = Sexp.n

        def emit_attn_loop2(tp):
            par = tp % 2
            par4 = tp % 4
            cs, lo, hi, sexp_at = attn_state.pop(tp)
            vee_at = {}

            def emit_av(h):
                hp = h % 2
                bk2 = 0
                off = (h % 4) * 65
                for ci, c in enumerate(cs):
                    ks = (tp + c - 1) % 6
                    wl = []
                    if ci == 0:
                        wl = [(Vev, vee_at[h])] + vaug_ready[ks]
                        if h % 4 == 0:
                            wl += P.wait_free(("pA", bk2))
                    E("tensor", (lambda e, bk2=bk2, off=off, hp=hp, c=c, ks=ks,
                                 h=h, ci=ci, ncs=len(cs): e.matmul(
                        out=pA[bk2][:, off:off + 65],
                        lhsT=expT[hp][:, c * 128:(c + 1) * 128],
                        rhs=vaug[ks][:, h * 65:(h + 1) * 65],
                        start=(ci == 0), stop=(ci == ncs - 1))),
                      waits=wl,
                      incs=([(Tav, 1)] if ci == len(cs) - 1 else []))
                if h % 4 == 3:
                    tav_h = Tav.n
                    for hh in range(h - 3, h + 1):
                        off2 = (hh % 4) * 65
                        rb = rcps[hh % 4]
                        E("vector", (lambda e, bk2=bk2, off2=off2, rb=rb:
                                     e.reciprocal(
                            out=rb[:], in_=pA[bk2][:, off2 + 64:off2 + 65])),
                          waits=[(Tav, tav_h), (Sac, Sac.n)],
                          reads=[f"rcp{hh % 4}"],
                          writes=[f"rcp{hh % 4}"], incs=[(Vrc, 1)])
                    vrc_h = Vrc.n
                    for hh in range(h - 3, h + 1):
                        off2 = (hh % 4) * 65
                        rb = rcps[hh % 4]
                        E("scalar", (lambda e, bk2=bk2, off2=off2, hh=hh,
                                     par=par, rb=rb: e.activation(
                            out=attn_tm[par][:, hh * 64:(hh + 1) * 64],
                            in_=pA[bk2][:, off2:off2 + 64], func=AF.Copy,
                            scale=rb[:])),
                          waits=[(Vrc, vrc_h)], writes=[f"attntm{par}"],
                          incs=[(Sac, 1)])
                    P.set_free(("pA", bk2), (Sac, Sac.n))

            for h in range(NH):
                hp = h % 2
                wl = [(Sexp, sexp_at[h])] + P.wait_free(("pT", hp))
                for ci, c in enumerate(cs):
                    E("tensor", (lambda e, h=h, hp=hp, c=c, par=par: e.transpose(
                        out=pT[hp][:, c * 128:(c + 1) * 128],
                        in_=exp_sb[par][h][:, c * 128:(c + 1) * 128],
                        identity=ident[:])),
                      waits=(wl if ci == 0 else []),
                      incs=([(Tp, 1)] if ci == len(cs) - 1 else []))
                exp_free[(par, h)] = [(Tp, Tp.n)]
                wl = [(Tp, Tp.n), (Tav, max(0, Tav.n - 1))]
                E("scalar", (lambda e, hp=hp, lo=lo, hi=hi: e.copy(
                    out=expT[hp][:, lo:hi], in_=pT[hp][:, lo:hi])),
                  waits=wl, incs=[(Vev, 1)], writes=[f"expT{hp}"])
                P.set_free(("pT", hp), (Vev, Vev.n))
                vee_at[h] = Vev.n
                if h >= 1:
                    emit_av(h - 1)
            emit_av(NH - 1)
            vav_all = Sac.n
            tp_block(
                [attn_tm[par][:, c * 128:(c + 1) * 128] for c in range(4)],
                [(attnT[par][:], f"attnT{par}")], [0], Tp, Vev,
                extra_first_waits=[(Sac, vav_all)])
            veat = Vev.n
            wl0 = [(Vev, veat), (Vqk, Vqk.n), (Vpa, Vpa.n)]
            for nb in range(2):
                for k in range(4):
                    E("tensor", (lambda e, nb=nb, k=k, par=par: e.matmul(
                        out=pQO[:, nb * 512:(nb + 1) * 512],
                        lhsT=attnT[par][:, k * 128:(k + 1) * 128],
                        rhs=wout_sb[k][:, nb * 512:(nb + 1) * 512],
                        start=(k == 0), stop=(k == 3))),
                      waits=(wl0 if (nb == 0 and k == 0) else []),
                      incs=([(Tout, 1)] if (nb == 1 and k == 3) else []))
            tout_h = Tout.n
            E("vector", (lambda e, par=par, sl2=tp % 6: e.scalar_tensor_tensor(
                out=pa_sb[par][:], in0=pQO[:], scalar=1.0, in1=xb[sl2][:],
                op0=OP.mult, op1=OP.add)),
              waits=[(Tout, tout_h), (StPa[par], StPa[par].n)],
              incs=[(Vpa, 1)], writes=[f"pasb{par}"])
            E("gpsimd", (lambda e, par=par, tp=tp: e.dma_start(
                out=pa_buf[tp * 128:(tp + 1) * 128, :], in_=pa_sb[par][:])),
              waits=[(Vpa, Vpa.n)], incs=[(StPa[par], 16)])
            P.set_free(("xb", tp % 6), (Vpa, Vpa.n))
            for c in cs:
                P.set_free(("krot", (tp + c - 1) % 6), (Tsim, Tsim.n))
                P.set_free(("vaug", (tp + c - 1) % 6), (Tav, Tav.n))
            P.set_free(("qrot", par4), (Tsim, Tsim.n))
            P.set_free(("qrotg", par4), (Tsim, Tsim.n))

        exp_free = {}
        tqkv_at = {}

        def emit_tail(tt):
            par = tt % 2
            par4 = tt % 4
            sl = tt % 6
            wl = [(Tqkv, tqkv_at[tt])] + P.wait_free(("qk_sb", par))
            E("scalar", (lambda e, par=par: e.copy(
                out=qk_sb[par][:], in_=pQO[:, 0:512])),
              waits=wl, writes=[f"qksb{par}"])
            wl = P.wait_free(("vaug", sl)) + [(Vms, 6)]
            E("scalar", (lambda e, sl=sl: e.copy(
                out=vaug[sl][:].rearrange("p (h c) -> p h c", c=65)[:, :, 0:64],
                in_=pQO[:, 512:1024].rearrange("p (h c) -> p h c", c=64))),
              waits=wl, incs=[(Vqk, 1)], writes=[f"vaug{sl}"])
            vqk_t = Vqk.n
            vaug_ready[sl] = [(Vqk, vqk_t)]
            wl = [(Vqk, vqk_t)] + P.wait_free(("qkraw", par))
            for h in range(NH):
                bk2 = h // 4
                E("tensor", (lambda e, h=h, par=par, bk2=bk2: e.transpose(
                    out=pT[bk2][0:64, (h % 4) * 128:((h % 4) + 1) * 128],
                    in_=qk_sb[par][:, h * 64:(h + 1) * 64], identity=ident[:])),
                  waits=((wl + P.wait_free(("pT", 0))) if h == 0 else
                         (P.wait_free(("pT", 1)) if h == 4 else [])),
                  incs=([(Tp, 1)] if h % 4 == 3 else []))
            tpq = Tp.n
            wl2 = P.wait_free(("ev", f"qkraw{par}"))
            for bk2 in range(2):
                co = bk2 * 512
                E("scalar", (lambda e, bk2=bk2, par=par, co=co: e.copy(
                    out=qkraw[par][0:64, co:co + 512], in_=pT[bk2][0:64, :])),
                  waits=[(Tp, tpq - 1 + bk2)] + (wl2 if bk2 == 0 else []),
                  writes=[f"qkraw{par}"],
                  incs=([(Vev, 1)] if bk2 == 1 else []))
                E("scalar", (lambda e, bk2=bk2, par=par, co=co: e.copy(
                    out=qkraw[par][64:96, co:co + 512], in_=pT[bk2][32:64, :])),
                  waits=[(Tp, tpq - 1 + bk2)] + (wl2 if bk2 == 0 else []),
                  writes=[f"qkraw{par}"])
                E("scalar", (lambda e, bk2=bk2, par=par, co=co: e.copy(
                    out=qkraw[par][96:128, co:co + 512], in_=pT[bk2][0:32, :])),
                  writes=[f"qkraw{par}"],
                  incs=([(Vev, 1)] if bk2 == 1 else []))
            veqk_t = Vev.n
            P.set_free(("pT", 0), (Vev, veqk_t))
            P.set_free(("pT", 1), (Vev, veqk_t))
            P.set_free(("qk_sb", par), (Tp, tpq))

            for vi in range(3):
                wl0 = [(Vev, veqk_t)]
                if vi == 0:
                    wl0 += P.wait_free(("qrot", par4)) + P.wait_free(("qrotg", par4))
                E("vector", (lambda e, vi=vi, par=par, par4=par4: e.tensor_tensor(
                    out=qrot[par4][vi][:], in0=qkraw[par][:],
                    in1=tabs[f"rq{vi}"][:], op=OP.mult)),
                  waits=wl0, writes=[f"qrot{par4}_{vi}"],
                  incs=[(Vrot, 1)])
            qrot_ready[par4] = [(Vrot, Vrot.n)]
            wl0 = P.wait_free(("krot", sl))
            E("vector", (lambda e, par=par: e.tensor_tensor(
                out=rtv[0][:], in0=qkraw[par][0:64, :], in1=tabs["rkc"][:],
                op=OP.mult)),
              waits=wl0, reads=["rtv0"], writes=["rtv0"])
            E("vector", (lambda e, par=par: e.tensor_tensor(
                out=rtv[1][:], in0=qkraw[par][64:128, :],
                in1=tabs["rks"][64:128, :], op=OP.mult)),
              reads=["rtv1"], writes=["rtv1"])
            E("vector", (lambda e, sl=sl: e.tensor_tensor(
                out=krot[sl][0:64, :], in0=rtv[0][:], in1=rtv[1][:],
                op=OP.add)),
              reads=["rtv0", "rtv1"], writes=[f"krot{sl}"])
            E("vector", (lambda e, sl=sl: e.tensor_copy(
                out=krot[sl][64:128, :], in_=krot[sl][0:64, :])),
              reads=[f"krot{sl}"], writes=[f"krot{sl}"], incs=[(Vrot, 1)])
            krot_ready[sl] = [(Vrot, Vrot.n)]
            P.set_free(("qkraw", par), (Vrot, Vrot.n))


        for t in range(NT):
            par = t % 2
            par4 = t % 4
            sl = t % 6
            xsl = t % 6
            w = t % wps
            stt = stats[par]
            skey = f"stats{par}"
            if t >= 1:
                emit_tail(t - 1)
            wl = [x_loaded[t]] + P.wait_free(("stats", par))
            E("vector", (lambda e, stt=stt, sl=xsl: e.bn_stats(
                out=stt[:, 0:6], in_=xb[sl][:, 0:512])),
              waits=wl, reads=[skey + "mv"], writes=[skey + "bn"])
            E("vector", (lambda e, stt=stt, sl=xsl: e.bn_stats(
                out=stt[:, 6:12], in_=xb[sl][:, 512:1024])),
              writes=[skey + "bn"])
            E("vector", (lambda e, stt=stt: e.bn_aggr(
                out=stt[:, 12:14], in_=stt[:, 0:12])),
              reads=[skey + "bn"], writes=[skey + "mv"])
            E("vector", (lambda e, stt=stt: e.tensor_scalar(
                out=stt[:, 14:15], in0=stt[:, 13:14], scalar1=EPS, scalar2=None,
                op0=OP.add)), reads=[skey + "mv"], writes=[skey + "va"])
            E("vector", (lambda e, stt=stt: e.tensor_scalar(
                out=stt[:, 15:16], in0=stt[:, 13:14], scalar1=1.0 + EPS,
                scalar2=EPS * EPS, op0=OP.mult, op1=OP.add)),
              writes=[skey + "va"], incs=[(Vln, 1)])
            vln_t = Vln.n
            E("scalar", (lambda e, stt=stt: e.activation(
                out=stt[:, 16:18], in_=stt[:, 14:16], func=AF.Ln)),
              waits=[(Vln, vln_t)], writes=[skey + "s"])
            E("scalar", (lambda e, stt=stt: e.activation(
                out=stt[:, 18:20], in_=stt[:, 16:18], func=AF.Exp,
                scale=-0.5)),
              reads=[skey + "s"], writes=[skey + "s"], incs=[(Ssq, 1)])
            ssq_t = Ssq.n
            E("vector", (lambda e, stt=stt: e.tensor_tensor(
                out=stt[:, 22:24], in0=stt[:, 18:20], in1=stt[:, 18:20],
                op=OP.mult)),
              waits=[(Ssq, ssq_t)], reads=[skey + "tmp"], writes=[skey + "tmp"])
            E("vector", (lambda e, stt=stt: e.tensor_tensor(
                out=stt[:, 22:24], in0=stt[:, 22:24], in1=stt[:, 14:16],
                op=OP.mult)), reads=[skey + "tmp"], writes=[skey + "tmp"])
            E("vector", (lambda e, stt=stt: e.tensor_scalar(
                out=stt[:, 22:24], in0=stt[:, 22:24], scalar1=-0.5,
                scalar2=1.5, op0=OP.mult, op1=OP.add)),
              reads=[skey + "tmp"], writes=[skey + "tmp"])
            E("vector", (lambda e, stt=stt: e.tensor_tensor(
                out=stt[:, 20:22], in0=stt[:, 22:24], in1=stt[:, 18:20],
                op=OP.mult)),
              reads=[skey + "tmp"], writes=[skey + "r"], incs=[(Vst2, 1)])
            vst2_t = Vst2.n
            if t >= 3:
                emit_attn_loop1(t - 3)
            E("vector", (lambda e, stt=stt, sl=xsl, par=par: e.tensor_scalar(
                out=xn_sb[par][:], in0=xb[sl][:], scalar1=stt[:, 12:13],
                scalar2=stt[:, 20:21], op0=OP.subtract, op1=OP.mult)),
              waits=[(StXn[par], StXn[par].n)],
              reads=[skey + "r", skey + "mv"], incs=[(Sxn, 1)],
              writes=[f"xn{par}"])
            sxn_t = Sxn.n
            wlx2 = P.wait_free(("xn2b", par))
            E("vector", (lambda e, stt=stt, sl=xsl, par=par: e.tensor_scalar(
                out=xn2b[par][:], in0=xb[sl][:], scalar1=stt[:, 12:13],
                scalar2=stt[:, 21:22], op0=OP.subtract, op1=OP.mult)),
              waits=wlx2, incs=[(Sxn2, 1)], writes=[f"xn2b{par}"])
            sxn2_t = Sxn2.n
            P.set_free(("stats", par), (Sxn2, sxn2_t))
            E("gpsimd", (lambda e, par=par, t=t: e.dma_start(
                out=xn_buf[t * 128:(t + 1) * 128, :], in_=xn_sb[par][:])),
              waits=[(Sxn, sxn_t)], incs=[(StXn[par], 16)])
            if t >= 4:
                emit_attn_loop2(t - 4)
            wl = [(Sxn2, sxn2_t)] + P.wait_free(("xn2T", par))
            tp_block(
                [xn2b[par][:, c * 128:(c + 1) * 128] for c in range(8)],
                [(xn2T[par][:, 0:512], f"xn2T{par}a"),
                 (xn2T[par][:, 512:1024], f"xn2T{par}b")],
                [0, 1], Tp, Vev, extra_first_waits=wl)
            vex_t = Vev.n
            P.set_free(("xn2b", par), (Tp, Tp.n))
            wl = [(Vev, vex_t), (Vpa, Vpa.n), (Vqp, vqp_prep), (TbeC, 8),
                  (LdC, ldc_all)]
            for nb in range(2):
                for k in range(8):
                    E("tensor", (lambda e, nb=nb, k=k, par=par: e.matmul(
                        out=pQO[:, nb * 512:(nb + 1) * 512],
                        lhsT=xn2T[par][:, k * 128:(k + 1) * 128],
                        rhs=wqkv_sb[k][:, nb * 512:(nb + 1) * 512],
                        start=(k == 0), stop=False)),
                      waits=(wl if (nb == 0 and k == 0) else []))
                E("tensor", (lambda e, nb=nb: e.matmul(
                    out=pQO[:, nb * 512:(nb + 1) * 512], lhsT=onesrowb[:],
                    rhs=beff[0:1, nb * 512:(nb + 1) * 512],
                    start=False, stop=True)),
                  incs=([(Tqkv, 1)] if nb == 1 else []))
            tqkv_at[t] = Tqkv.n
            P.set_free(("xn2T", par), (Tqkv, Tqkv.n))
            if t + 2 < NT:
                wl = P.wait_free(("xb", (t + 2) % 6))
                sl2 = (t + 2) % 6
                E("sync", (lambda e, t2=t + 2: e.dma_start(
                    out=xb[t2 % 6][:], in_=x_in[t2 * 128:(t2 + 1) * 128, :])),
                  waits=wl, incs=[(LdX[sl2], 16)])
                x_loaded[t + 2] = (LdX[sl2], LdX[sl2].n)

        emit_tail(NT - 1)
        if NT >= 4:
            emit_attn_loop2(NT - 4)
        for tp in range(max(0, NT - 3), NT):
            emit_attn_loop1(tp)
            emit_attn_loop2(tp)

        E("sync", lambda e: e.nop(),
          waits=[(StXn[0], StXn[0].n), (StXn[1], StXn[1].n),
                 (StPa[0], StPa[0].n), (StPa[1], StPa[1].n)] + stwq_all)
        P.barrier("barAB")

        ast.close()
        bstk = ExitStack()
        sb = lambda name, shape, dt: bstk.enter_context(nc.sbuf_tensor(name, shape, dt))
        ps = lambda name, shape, dt: bstk.enter_context(nc.psum_tensor(name, shape, dt))
        wgq = [sb(f"wgq{i}", [128, DI], BF16) for i in range(8)]
        wuq = [sb(f"wuq{i}", [128, DI], BF16) for i in range(8)]
        xn_in = [sb(f"xnin{i}", [128, D], F32) for i in range(2)]
        xq_b = [sb(f"xqb{i}", [128, D], BF16) for i in range(2)]
        xqT = [sb(f"xqT{i}", [128, D], BF16) for i in range(2)]
        g_sb = [sb(f"gsb{i}", [128, 512], F32) for i in range(2)]
        h_sb = sb("h_sb", [128, DI], F32)
        hq_sb = sb("hq_sb", [128, DI], BF16)
        bst = [sb(f"bstA{i}", [128, 16], F32) for i in range(2)]

        pG = [ps(f"pG{i}", [128, 512], F32) for i in range(2)]
        pU = [ps(f"pU{i}", [128, 512], F32) for i in range(2)]

        LdWg = S("LdWg")
        LdXn = [S(f"LdXn{i}") for i in range(2)]
        Vam = S("Vam"); Vxq = S("Vxq")
        Tg = S("Tg"); Tu = S("Tu"); Ssil = S("Ssil"); Vh = S("Vh")
        Vsh = S("Vsh"); Gq = S("Gq"); StHq = S("StHq")

        for k in range(8):
            E("sync", (lambda e, k=k: e.dma_start(
                out=wgq[k][:], in_=wqg_buf[k * 128:(k + 1) * 128, :])),
              incs=[(LdWg, 16)])
            E("sync", (lambda e, k=k: e.dma_start(
                out=wuq[k][:], in_=wqu_buf[k * 128:(k + 1) * 128, :])),
              incs=[(LdWg, 16)])
        ldwg_all = LdWg.n
        xn_loaded = {}
        E("sync", (lambda e: e.dma_start(
            out=xn_in[0][:], in_=xn_buf[0:128, :])), incs=[(LdXn[0], 16)])
        xn_loaded[0] = (LdXn[0], 16)

        vexq_at = {}
        vam_at = {}
        vxq_at = {}

        def emit_xq(tt):
            par = tt % 2
            stt = bst[par]
            skey = f"bstA{par}"
            wl = [xn_loaded[tt]] + P.wait_free(("bst", par))
            E("vector", (lambda e, stt=stt, par=par: e.tensor_reduce(
                out=stt[:, 0:1], in_=xn_in[par][:], axis=AX.X, op=OP.max,
                apply_absolute_value=True)),
              waits=wl, reads=[skey], writes=[skey])
            E("vector", (lambda e, stt=stt: e.tensor_scalar(
                out=stt[:, 1:2], in0=stt[:, 0:1], scalar1=1e-5,
                scalar2=1.0 / 127.0, op0=OP.max, op1=OP.mult)),
              reads=[skey], writes=[skey])
            E("vector", (lambda e, stt=stt: e.reciprocal(
                out=stt[:, 2:3], in_=stt[:, 1:2])),
              reads=[skey], writes=[skey])
            E("vector", (lambda e, stt=stt: e.tensor_scalar(
                out=stt[:, 3:4], in0=invwbc[:, 0:1], scalar1=stt[:, 1:2],
                scalar2=None, op0=OP.mult)), reads=[skey], writes=[skey])
            E("vector", (lambda e, stt=stt: e.tensor_scalar(
                out=stt[:, 4:5], in0=invwbc[:, 1:2], scalar1=stt[:, 1:2],
                scalar2=None, op0=OP.mult)),
              reads=[skey], writes=[skey], incs=[(Vam, 1)])
            vam_at[tt] = Vam.n
            E("vector", (lambda e, stt=stt, par=par: e.tensor_scalar(
                out=xn_in[par][:], in0=xn_in[par][:], scalar1=stt[:, 2:3],
                scalar2=MAGIC, op0=OP.mult, op1=OP.add)),
              reads=[f"xnin{par}", skey], writes=[f"xnin{par}"])
            E("vector", (lambda e, par=par: e.tensor_scalar(
                out=xn_in[par][:], in0=xn_in[par][:], scalar1=MAGIC,
                scalar2=127.0, op0=OP.subtract, op1=OP.min)),
              reads=[f"xnin{par}"], writes=[f"xnin{par}"])
            wl = P.wait_free(("xq_b", par))
            E("vector", (lambda e, par=par: e.tensor_scalar(
                out=xq_b[par][:], in0=xn_in[par][:], scalar1=-128.0,
                scalar2=None, op0=OP.max)),
              waits=wl, reads=[f"xnin{par}"], writes=[f"xqb{par}"],
              incs=[(Vxq, 1)])
            vxq_t = Vxq.n
            P.set_free(("xn_in", par), (Vxq, vxq_t))
            vxq_at[tt] = vxq_t

        def emit_xqT(tt):
            par = tt % 2
            wl = [(Vxq, vxq_at[tt]), (LdWg, ldwg_all)] \
                + P.wait_free(("xqT", par))
            tp_block(
                [xq_b[par][:, c * 128:(c + 1) * 128] for c in range(8)],
                [(xqT[par][:, 0:512], f"xqT{par}a"),
                 (xqT[par][:, 512:1024], f"xqT{par}b")],
                [0, 1], Tp, Vev, extra_first_waits=wl)
            vexq_at[tt] = Vev.n
            P.set_free(("xq_b", par), (Tp, Tp.n))

        emit_xq(0)
        emit_xqT(0)

        for t in range(NT):
            par = t % 2
            stt = bst[par]
            skey = f"bstA{par}"
            if t + 1 < NT:
                pr2 = (t + 1) % 2
                wl = P.wait_free(("xn_in", pr2))
                E("sync", (lambda e, t2=t + 1: e.dma_start(
                    out=xn_in[t2 % 2][:],
                    in_=xn_buf[t2 * 128:(t2 + 1) * 128, :])),
                  waits=wl, incs=[(LdXn[pr2], 16)])
                xn_loaded[t + 1] = (LdXn[pr2], LdXn[pr2].n)
                emit_xq(t + 1)
            vexq_t = vexq_at[t]
            vam_t = vam_at[t]
            for n in range(8):
                np_ = n % 2
                wlg = [(Vev, vexq_t), (Ssil, max(0, Ssil.n - 1))]
                for k in range(8):
                    E("tensor", (lambda e, n=n, k=k, par=par, np_=np_: e.matmul(
                        out=pG[np_][:], lhsT=xqT[par][:, k * 128:(k + 1) * 128],
                        rhs=wgq[k][:, n * 512:(n + 1) * 512],
                        start=(k == 0), stop=(k == 7))),
                      waits=(wlg if k == 0 else []),
                      incs=([(Tg, 1)] if k == 7 else []))
                tg_n = Tg.n
                wlu = [(Vh, max(0, Vh.n - 1))]
                for k in range(8):
                    E("tensor", (lambda e, n=n, k=k, par=par, np_=np_: e.matmul(
                        out=pU[np_][:], lhsT=xqT[par][:, k * 128:(k + 1) * 128],
                        rhs=wuq[k][:, n * 512:(n + 1) * 512],
                        start=(k == 0), stop=(k == 7))),
                      waits=(wlu if k == 0 else []),
                      incs=([(Tu, 1)] if k == 7 else []))
                tu_n = Tu.n
                E("scalar", (lambda e, np_=np_, stt=stt: e.activation(
                    out=g_sb[np_][:], in_=pG[np_][:], func=AF.Silu,
                    scale=stt[:, 3:4])),
                  waits=[(Tg, tg_n), (Vam, vam_t), (Vh, max(0, Vh.n - 1))],
                  incs=[(Ssil, 1)], writes=[f"gsb{np_}"])
                ssil_n = Ssil.n
                wlh = [(Tu, tu_n), (Ssil, ssil_n)]
                if n == 0:
                    wlh += P.wait_free(("h_sb",))
                E("vector", (lambda e, n=n, np_=np_, stt=stt:
                             e.scalar_tensor_tensor(
                    out=h_sb[:, n * 512:(n + 1) * 512], in0=pU[np_][:],
                    scalar=stt[:, 4:5], in1=g_sb[np_][:],
                    op0=OP.mult, op1=OP.mult)),
                  waits=wlh, reads=["h_sb"], writes=["h_sb"], incs=[(Vh, 1)])
            if t + 1 < NT:
                emit_xqT(t + 1)
            E("vector", (lambda e, stt=stt: e.tensor_reduce(
                out=stt[:, 5:6], in_=h_sb[:], axis=AX.X, op=OP.max,
                apply_absolute_value=True)),
              reads=["h_sb"], writes=[skey])
            E("vector", (lambda e, stt=stt: e.tensor_scalar(
                out=stt[:, 6:7], in0=stt[:, 5:6], scalar1=1e-5,
                scalar2=1.0 / 127.0, op0=OP.max, op1=OP.mult)),
              reads=[skey], writes=[skey])
            E("vector", (lambda e, stt=stt: e.reciprocal(
                out=stt[:, 7:8], in_=stt[:, 6:7])),
              reads=[skey], writes=[skey])
            E("vector", (lambda e, stt=stt, t=t: e.tensor_scalar(
                out=combs[:, t:t + 1], in0=invwbc[:, 2:3], scalar1=stt[:, 6:7],
                scalar2=None, op0=OP.mult)),
              reads=[skey], writes=["combs"], incs=[(Vsh, 1)])
            vsh_t = Vsh.n
            P.set_free(("bst", par), (Vsh, vsh_t))
            E("vector", (lambda e, stt=stt: e.tensor_scalar(
                out=h_sb[:], in0=h_sb[:], scalar1=stt[:, 7:8], scalar2=MAGIC,
                op0=OP.mult, op1=OP.add)),
              waits=[(Vsh, vsh_t)], reads=["h_sb"], writes=["h_sb"],
              incs=[(Vh, 1)])
            vh_t = Vh.n
            E("vector", (lambda e: e.tensor_scalar(
                out=h_sb[:], in0=h_sb[:], scalar1=MAGIC, scalar2=127.0,
                op0=OP.subtract, op1=OP.min)),
              reads=["h_sb"], writes=["h_sb"])
            E("vector", (lambda e: e.tensor_scalar(
                out=hq_sb[:], in0=h_sb[:], scalar1=-128.0, scalar2=None,
                op0=OP.max)),
              waits=[(StHq, StHq.n)],
              reads=["h_sb"], writes=["hq_sb"], incs=[(Gq, 1)])
            gq_t = Gq.n
            P.set_free(("h_sb",), (Gq, gq_t))
            E("gpsimd", (lambda e, t=t: e.dma_start(
                out=hq_buf[t * 128:(t + 1) * 128, :], in_=hq_sb[:])),
              waits=[(Gq, gq_t)], incs=[(StHq, 16)])

        E("sync", lambda e: e.nop(), waits=[(StHq, 16 * NT)])
        P.barrier("barBC")

        bstk.close()
        cstk = ExitStack()
        sb = lambda name, shape, dt: cstk.enter_context(nc.sbuf_tensor(name, shape, dt))
        ps = lambda name, shape, dt: cstk.enter_context(nc.psum_tensor(name, shape, dt))
        wdq = [sb(f"wdq{i}", [128, D], BF16) for i in range(32)]
        hq_in = [sb(f"hqin{i}", [128, DI], BF16) for i in range(2)]
        hqT = [sb(f"hqTt{i}", [128, DI], BF16) for i in range(2)]
        pa_in = [sb(f"pain{i}", [128, D], F32) for i in range(2)]
        y_sb = [sb(f"ysb{i}", [128, D], F32) for i in range(2)]

        pD = [ps(f"pD{i}", [128, 1024], F32) for i in range(2)]

        LdWd = S("LdWd")
        LdHq = [S(f"LdHq{i}") for i in range(2)]
        LdPa = [S(f"LdPa{i}") for i in range(2)]
        Td = S("Td"); Vy = S("Vy")
        StY = [S(f"StY{i}") for i in range(2)]

        for k in range(32):
            E("sync", (lambda e, k=k: e.dma_start(
                out=wdq[k][:], in_=wqd_buf[k * 128:(k + 1) * 128, :])),
              incs=[(LdWd, 16)])
        ldwd_all = LdWd.n
        hq_loaded = {}
        pa_loaded = {}
        E("sync", (lambda e: e.dma_start(
            out=hq_in[0][:], in_=hq_buf[0:128, :])), incs=[(LdHq[0], 16)])
        hq_loaded[0] = (LdHq[0], 16)
        E("sync", (lambda e: e.dma_start(
            out=pa_in[0][:], in_=pa_buf[0:128, :])), incs=[(LdPa[0], 16)])
        pa_loaded[0] = (LdPa[0], 16)

        vehq_at = {}

        def emit_hqT(tt):
            par = tt % 2
            wl = [hq_loaded[tt], (LdWd, ldwd_all)] + P.wait_free(("hqT", par))
            tp_block(
                [hq_in[par][:, c * 128:(c + 1) * 128] for c in range(32)],
                [(hqT[par][:, b * 512:(b + 1) * 512], f"hqT{par}_{b}")
                 for b in range(8)],
                [0, 1, 0, 1, 0, 1, 0, 1], Tp, Vev, extra_first_waits=wl,
                evac_eng="vector")
            vehq_at[tt] = Vev.n
            P.set_free(("hq_in", par), (Tp, Tp.n))

        emit_hqT(0)

        for t in range(NT):
            par = t % 2
            if t + 1 < NT:
                pr2 = (t + 1) % 2
                wl = P.wait_free(("hq_in", pr2))
                E("sync", (lambda e, t2=t + 1: e.dma_start(
                    out=hq_in[t2 % 2][:],
                    in_=hq_buf[t2 * 128:(t2 + 1) * 128, :])),
                  waits=wl, incs=[(LdHq[pr2], 16)])
                hq_loaded[t + 1] = (LdHq[pr2], LdHq[pr2].n)
                wl = P.wait_free(("pa_in", pr2))
                E("sync", (lambda e, t2=t + 1: e.dma_start(
                    out=pa_in[t2 % 2][:],
                    in_=pa_buf[t2 * 128:(t2 + 1) * 128, :])),
                  waits=wl, incs=[(LdPa[pr2], 16)])
                pa_loaded[t + 1] = (LdPa[pr2], LdPa[pr2].n)
                emit_hqT(t + 1)
            wl = [(Vev, vehq_at[t])] + P.wait_free(("pD", par))
            for nb in range(2):
                for k in range(32):
                    E("tensor", (lambda e, nb=nb, k=k, par=par: e.matmul(
                        out=pD[par][:, nb * 512:(nb + 1) * 512],
                        lhsT=hqT[par][:, k * 128:(k + 1) * 128],
                        rhs=wdq[k][:, nb * 512:(nb + 1) * 512],
                        start=(k == 0), stop=(k == 31))),
                      waits=(wl if (nb == 0 and k == 0) else []),
                      incs=([(Td, 1)] if (nb == 1 and k == 31) else []))
            td_t = Td.n
            P.set_free(("hqT", par), (Td, td_t))
            E("vector", (lambda e, par=par, t=t: e.scalar_tensor_tensor(
                out=y_sb[par][:], in0=pD[par][:], scalar=combs[:, t:t + 1],
                in1=pa_in[par][:], op0=OP.mult, op1=OP.add)),
              waits=[(Td, td_t), pa_loaded[t],
                     (StY[par], StY[par].n)],
              incs=[(Vy, 1)], writes=[f"ysb{par}"])
            vy_t = Vy.n
            P.set_free(("pD", par), (Vy, vy_t))
            P.set_free(("pa_in", par), (Vy, vy_t))
            E("gpsimd", (lambda e, par=par, t=t: e.dma_start(
                out=y_out[t * 128:(t + 1) * 128, :], in_=y_sb[par][:])),
              waits=[(Vy, vy_t)], incs=[(StY[par], 16)])

        E("sync", lambda e: e.nop(),
          waits=[(StY[0], StY[0].n), (StY[1], StY[1].n)])
        cstk.close()

        with nc.Block() as blk:
            P.replay(blk)
    return nc


_NC_CACHE = {}


def kernel(L0_attn_x, L0_attn_ln_g, L0_attn_ln_b, L0_attn_w_qkv, L0_attn_w_out,
           L0_ffn_w_gate, L0_ffn_w_up, L0_ffn_w_down):
    n_seq, wps = 2, 32
    x = np.ascontiguousarray(np.asarray(L0_attn_x, dtype=np.float32))
    B, Sq, Dm = x.shape
    cores = 8
    key = (n_seq, wps)
    if key not in _NC_CACHE:
        _NC_CACHE[key] = build_nc(n_seq, wps)
    nc = _NC_CACHE[key]

    consts = host_consts()
    shared = {
        "wqkvT": np.ascontiguousarray(np.asarray(L0_attn_w_qkv, np.float32).T),
        "woutT": np.ascontiguousarray(np.asarray(L0_attn_w_out, np.float32).T),
        "wgT": np.ascontiguousarray(np.asarray(L0_ffn_w_gate, np.float32).T),
        "wuT": np.ascontiguousarray(np.asarray(L0_ffn_w_up, np.float32).T),
        "wdT": np.ascontiguousarray(np.asarray(L0_ffn_w_down, np.float32).T),
        "g_pc": np.ascontiguousarray(
            np.asarray(L0_attn_ln_g, np.float32).reshape(8, 128).T),
        "b_pc": np.ascontiguousarray(
            np.asarray(L0_attn_ln_b, np.float32).reshape(8, 128).T),
    }
    for k, v in consts.items():
        shared[k] = np.ascontiguousarray(v)

    in_maps = []
    for c in range(cores):
        m = dict(shared)
        m["x"] = np.ascontiguousarray(
            x[c * n_seq:(c + 1) * n_seq].reshape(n_seq * Sq, Dm))
        in_maps.append(m)

    res = run_bass_kernel_spmd(nc, in_maps, core_ids=list(range(cores)))
    out = np.concatenate(
        [np.asarray(r["y"], np.float32).reshape(n_seq, Sq, Dm)
         for r in res.results], axis=0)
    return out


# version 20
# speedup vs baseline: 1.0562x; 1.0562x over previous
import sys
sys.path.insert(0, '/opt/trn_rl_repo')
import numpy as np
import ml_dtypes
from contextlib import ExitStack

import concourse.bass as bass
import concourse.mybir as mybir
from concourse.alu_op_type import AluOpType as OP
from concourse.bass_utils import run_bass_kernel_spmd

F32 = mybir.dt.float32
BF16 = mybir.dt.bfloat16
AF = mybir.ActivationFunctionType
AX = mybir.AxisListType
bf16 = ml_dtypes.bfloat16

MAGIC = 12582912.0
NEGM = -3.0e38
EPS = 1e-5
DH = 64
WS = 128
NH = 8
D = 1024
DI = 4096

ENGS = ("sync", "vector", "scalar", "tensor", "gpsimd")


def _base_tables():
    f = np.arange(0, DH, 2, dtype=np.float64) / DH
    inv_freq = 1.0 / 10000.0 ** f
    sbase = (np.arange(0, DH, 2, dtype=np.float64) + 0.4 * DH) / (1.4 * DH)
    return inv_freq, sbase


def _rot_tables():
    inv_freq, sbase = _base_tables()
    i = np.arange(WS, dtype=np.float64)
    out = {}
    for c in range(3):
        ang = np.outer(256.0 + i - 128.0 * (c - 1), inv_freq)
        expo = (64.0 + i + 128.0 * (1 - c)) / (WS // 2)
        scale = sbase[None, :] ** expo[:, None]
        cc = np.cos(ang) * scale * 0.125
        ss = np.sin(ang) * scale * 0.125
        out[f"rqc{c}"] = np.concatenate([cc, cc], 1).T
        out[f"rqs{c}"] = np.concatenate([-ss, ss], 1).T
    ang = np.outer(128.0 + i, inv_freq)
    expo = -(i - 64.0) / (WS // 2)
    scale = sbase[None, :] ** expo[:, None]
    cc = np.cos(ang) * scale
    ss = np.sin(ang) * scale
    out["rkc"] = np.concatenate([cc, cc], 1).T
    out["rks"] = np.concatenate([-ss, ss], 1).T
    res = {}
    for c in range(3):
        cc = np.tile(out[f"rqc{c}"], (1, NH)).astype(bf16)
        ss = np.tile(out[f"rqs{c}"], (1, NH)).astype(bf16)
        res[f"rq{c}"] = np.concatenate([cc, ss], axis=0)
    res["rkc"] = np.tile(out["rkc"], (1, NH)).astype(bf16)
    kss = np.tile(out["rks"], (1, NH)).astype(bf16)
    res["rks"] = np.concatenate([np.zeros_like(kss), kss], axis=0)
    return res


def host_consts():
    c = _rot_tables()
    i = np.arange(WS)[:, None]
    jc = np.arange(WS)[None, :]
    c["mask0"] = np.where(jc >= i, 0.0, NEGM).astype(bf16)
    c["mask2"] = np.where(jc <= i, 0.0, NEGM).astype(bf16)
    c["identb"] = np.eye(128, dtype=bf16)
    c["onescol"] = np.ones((128, 1), dtype=np.float32)
    c["onesrow"] = np.ones((1, 128), dtype=np.float32)
    c["onesrowb"] = np.ones((1, 128), dtype=bf16)
    return c


class Sem:
    __slots__ = ("h", "n", "name")

    def __init__(self, h, name):
        self.h = h
        self.n = 0
        self.name = name


class Prog:
    def __init__(self, nc, stack):
        self.nc = nc
        self.stack = stack
        self.q = {e: [] for e in ENGS}
        self.wm = {e: {} for e in ENGS}
        self.free = {}
        self.all_sems = []
        self.chain = {}
        self.wsig = {e: {} for e in ENGS}
        self.rsig = {e: {} for e in ENGS}

    def sem(self, name):
        s = Sem(self.stack.enter_context(self.nc.semaphore(name)), name)
        self.all_sems.append(s)
        return s

    def emit(self, eng, fn, waits=(), incs=(), reads=(), writes=()):
        wl_raw = list(waits)
        for r in reads:
            sig = self.wsig[eng].get(r)
            if sig:
                wl_raw.append(sig)
        for w in writes:
            for d in (self.wsig[eng], self.rsig[eng]):
                sig = d.get(w)
                if sig:
                    wl_raw.append(sig)
        wl = []
        for s, v in wl_raw:
            if v <= 0:
                continue
            if self.wm[eng].get(s.name, -1) >= v:
                continue
            self.wm[eng][s.name] = v
            wl.append((s.h, v))
        incs = list(incs)
        if not incs and (reads or writes):
            if eng not in self.chain:
                self.chain[eng] = self.sem(f"chain_{eng}")
            incs = [(self.chain[eng], 1)]
        for s, k in incs:
            s.n += k
        inc_l = [(s.h, k) for s, k in incs]
        mysig = (incs[0][0], incs[0][0].n) if incs else None
        if mysig:
            for w in writes:
                self.wsig[eng][w] = mysig
            for r in reads:
                self.rsig[eng][r] = mysig
        self.q[eng].append((False, wl, fn, inc_l))

    def wait_free(self, key):
        return self.free.pop(key, [])

    def set_free(self, key, *sv):
        self.free[key] = [(s, v) for s, v in sv]

    def replay(self, blk):
        def mk(steps):
            def body(e):
                for need_drain, wl, fn, incl in steps:
                    if need_drain:
                        e.drain()
                    for h, v in wl:
                        e.wait_ge(h, v)
                    inst = fn(e)
                    for h, k in incl:
                        inst.then_inc(h, k)
            return body
        for eng in ENGS:
            getattr(blk, eng)(mk(self.q[eng]))

    def barrier(self, name):
        bar = self.sem(name)
        for eng in ENGS:
            self.emit(eng, lambda e: e.nop(), incs=[(bar, 1)])
        for eng in ENGS:
            self.emit(eng, lambda e: e.nop(), waits=[(bar, 5)])


def build_nc(n_seq, wps):
    NT = n_seq * wps
    TOK = NT * 128
    nc = bass.Bass("TRN2", target_bir_lowering=False)
    consts = host_consts()

    x_in = nc.dram_tensor("x", [TOK, D], F32, kind="ExternalInput").ap()
    wqkvT = nc.dram_tensor("wqkvT", [D, D], F32, kind="ExternalInput").ap()
    woutT = nc.dram_tensor("woutT", [512, D], F32, kind="ExternalInput").ap()
    wgT = nc.dram_tensor("wgT", [D, DI], F32, kind="ExternalInput").ap()
    wuT = nc.dram_tensor("wuT", [D, DI], F32, kind="ExternalInput").ap()
    wdT = nc.dram_tensor("wdT", [DI, D], F32, kind="ExternalInput").ap()
    g_pc = nc.dram_tensor("g_pc", [128, 8], F32, kind="ExternalInput").ap()
    b_pc = nc.dram_tensor("b_pc", [128, 8], F32, kind="ExternalInput").ap()
    cap = {}
    for k, v in consts.items():
        dt = BF16 if v.dtype == bf16 else F32
        cap[k] = nc.dram_tensor(k, list(v.shape), dt, kind="ExternalInput").ap()
    y_out = nc.dram_tensor("y", [TOK, D], F32, kind="ExternalOutput").ap()

    xn_buf = nc.dram_tensor("xn_buf", [TOK, D], F32, kind="Internal").ap()
    pa_buf = nc.dram_tensor("pa_buf", [TOK, D], F32, kind="Internal").ap()
    hq_buf = nc.dram_tensor("hq_buf", [TOK, DI], BF16, kind="Internal").ap()
    wqg_buf = nc.dram_tensor("wqg_buf", [D, DI], BF16, kind="Internal").ap()
    wqu_buf = nc.dram_tensor("wqu_buf", [D, DI], BF16, kind="Internal").ap()
    wqd_buf = nc.dram_tensor("wqd_buf", [DI, D], BF16, kind="Internal").ap()

    st = ExitStack()
    with st:
        P = Prog(nc, st)
        E = P.emit
        S = P.sem
        sb = lambda name, shape, dt: st.enter_context(nc.sbuf_tensor(name, shape, dt))
        ps = lambda name, shape, dt: st.enter_context(nc.psum_tensor(name, shape, dt))

        ident = sb("ident", [128, 128], BF16)
        onescol = sb("onescol_s", [128, 1], F32)
        onesrow = sb("onesrow_s", [1, 128], F32)
        onesrowb = sb("onesrowb_s", [1, 128], BF16)
        invwbc = sb("invwbc", [128, 4], F32)
        combs = sb("combs", [128, NT], F32)
        beff = sb("beff", [1, D], BF16)
        gsbuf = sb("gsbuf", [128, 8], F32)
        bsbuf = sb("bsbuf", [128, 8], F32)

        ast = ExitStack()
        sb = lambda name, shape, dt: ast.enter_context(nc.sbuf_tensor(name, shape, dt))
        ps = lambda name, shape, dt: ast.enter_context(nc.psum_tensor(name, shape, dt))
        tabs = {k: sb("t_" + k, list(consts[k].shape),
                      BF16 if consts[k].dtype == bf16 else F32)
                for k in ("rq0", "rq1", "rq2", "rkc", "rks",
                          "mask0", "mask2")}
        xb = [sb(f"xb{i}", [128, D], F32) for i in range(6)]
        xn_sb = [sb(f"xn{i}", [128, D], F32) for i in range(2)]
        xn2b = [sb(f"xn2b{i}", [128, D], BF16) for i in range(2)]
        xn2T = [sb(f"xn2T{i}", [128, D], BF16) for i in range(2)]
        qk_sb = [sb(f"qksb{i}", [128, 512], BF16) for i in range(2)]
        qkraw = [sb(f"qkraw{i}", [128, 1024], BF16) for i in range(2)]
        krot = [sb(f"krot{i}", [128, 1024], BF16) for i in range(6)]
        vaug = [sb(f"vaug{i}", [128, 8 * 65], BF16) for i in range(6)]
        qrot = [[sb(f"qrot{p}_{c}", [128, 1024], BF16) for c in range(3)]
                for p in range(4)]
        rtv = [sb(f"rtv{i}", [64, 1024], BF16) for i in range(2)]
        exp_sb = [[sb(f"exp{p}_{i}", [128, 384], BF16) for i in range(8)]
                  for p in range(2)]
        expT = [sb(f"expT{i}", [128, 384], BF16) for i in range(2)]
        attn_tm = [sb(f"attntm{i}", [128, 512], BF16) for i in range(2)]
        attnT = [sb(f"attnT{i}", [128, 512], BF16) for i in range(2)]
        pa_sb = [sb(f"pasb{i}", [128, D], F32) for i in range(2)]
        stats = [sb(f"stats{i}", [128, 32], F32) for i in range(2)]
        nmax = [sb(f"nmax{i}", [128, 1], F32) for i in range(3)]
        rcp = sb("rcp", [128, 1], F32)
        rcps = [sb(f"rcpb{i}", [128, 1], F32) for i in range(4)]
        wqkv_sb = [sb(f"wqkv{i}", [128, D], BF16) for i in range(8)]
        wout_sb = [sb(f"wout{i}", [128, D], BF16) for i in range(4)]
        wstage = [sb(f"wstage{i}", [128, DI], F32) for i in range(2)]
        wqstage = [sb(f"wqstage{i}", [128, DI], BF16) for i in range(1)]
        wsums = sb("wsums", [128, 32], F32)
        wtot = sb("wtot", [1, 4], F32)

        pT = [st.enter_context(nc.psum_tensor(f"pT{i}", [128, 512], BF16))
              for i in range(2)]
        pQO = ps("pQO", [128, 1024], F32)
        pS = [ps(f"pS{i}", [128, 384], F32) for i in range(3)]
        pA = [ps(f"pA{i}", [128, 260], F32) for i in range(1)]

        LdC = S("LdC")
        LdX = [S(f"LdX{i}") for i in range(6)]
        StXn = [S(f"StXn{i}") for i in range(2)]
        StPa = [S(f"StPa{i}") for i in range(2)]
        Vln = S("Vln"); Ssq = S("Ssq"); Vst2 = S("Vst2")
        Sxn = S("Sxn"); Sxn2 = S("Sxn2")
        Tp = S("Tp"); Vev = S("Vev")
        Tqkv = S("Tqkv"); Vqk = S("Vqk"); Vrot = S("Vrot"); Grot = S("Grot")
        Tsim = S("Tsim"); Vmax = S("Vmax"); Sexp = S("Sexp")
        Tav = S("Tav"); Vav = S("Vav")
        Tout = S("Tout"); Vpa = S("Vpa"); Vrc = S("Vrc"); Sac = S("Sac")
        LdW = [S(f"LdW{i}") for i in range(2)]
        Vw1 = S("Vw1"); Tw = S("Tw"); Vw2 = S("Vw2"); Gw = S("Gw")
        StWq = [S(f"StWq{i}") for i in range(2)]
        LdQP = [S(f"LdQP{i}") for i in range(2)]
        TbeC = S("TbeC")
        Vqp = S("Vqp")

        def tp_block(in_tiles, out_slices, bank_ids, fillsem, evacsem,
                     extra_first_waits=(), evac_dtkey=None, part64=False,
                     evac_eng="scalar"):
            nb = len(bank_ids)
            for i, ap_in in enumerate(in_tiles):
                b = bank_ids[i // 4]
                wl = list(extra_first_waits) if i == 0 else []
                if i % 4 == 0:
                    wl += P.wait_free(("pT", b))
                dst = pT[b][0:64, (i % 4) * 128:((i % 4) + 1) * 128] if part64 \
                    else pT[b][:, (i % 4) * 128:((i % 4) + 1) * 128]
                E("tensor", (lambda e, dst=dst, ap_in=ap_in: e.transpose(
                    out=dst, in_=ap_in, identity=ident[:])),
                  waits=wl, incs=([(Tp, 1)] if i % 4 == 3 else []))
                if i % 4 == 3:
                    tp_here = Tp.n
                    out_ap, out_name = out_slices[i // 4]
                    src = pT[b][0:64, :] if part64 else pT[b][:]
                    wl2 = [(Tp, tp_here)] + P.wait_free(("ev", out_name))
                    if evac_eng == "scalar":
                        E("scalar", (lambda e, out_ap=out_ap, src=src: e.copy(
                            out=out_ap, in_=src)),
                          waits=wl2, incs=[(Vev, 1)], writes=[out_name])
                    else:
                        E("vector", (lambda e, out_ap=out_ap, src=src:
                                     e.tensor_copy(out=out_ap, in_=src)),
                          waits=wl2, incs=[(Vev, 1)], writes=[out_name])
                    P.set_free(("pT", b), (Vev, Vev.n))
            return Vev.n

        for k in tabs:
            E("sync", (lambda e, d=tabs[k], s=cap[k]: e.dma_start(out=d[:], in_=s)),
              incs=[(LdC, 16)])
        E("sync", lambda e: e.dma_start(out=ident[:], in_=cap["identb"]), incs=[(LdC, 16)])
        E("sync", lambda e: e.dma_start(out=onescol[:], in_=cap["onescol"]), incs=[(LdC, 16)])
        E("sync", lambda e: e.dma_start(out=onesrow[:], in_=cap["onesrow"]), incs=[(LdC, 16)])
        E("sync", lambda e: e.dma_start(out=onesrowb[:], in_=cap["onesrowb"]), incs=[(LdC, 16)])
        E("sync", lambda e: e.dma_start(out=gsbuf[:], in_=g_pc), incs=[(LdC, 16)])
        E("sync", lambda e: e.dma_start(out=bsbuf[:], in_=b_pc), incs=[(LdC, 16)])
        ldc_all = LdC.n
        Vms = S("Vms")
        for s0 in range(6):
            E("vector", (lambda e, v=vaug[s0]: e.memset(
                v[:].rearrange("p (h c) -> p h c", c=65)[:, :, 64:65], 1.0)),
              writes=[f"vaug{s0}"], incs=[(Vms, 1)])

        x_loaded = {}
        for t0 in range(min(2, NT)):
            E("sync", (lambda e, t0=t0: e.dma_start(
                out=xb[t0 % 6][:], in_=x_in[t0 * 128:(t0 + 1) * 128, :])),
              incs=[(LdX[t0 % 6], 16)])
            x_loaded[t0] = (LdX[t0 % 6], LdX[t0 % 6].n)

        for c in range(8):
            stg = wstage[c % 2]
            qp = LdQP[c % 2]
            E("sync", (lambda e, d=stg, s=wqkvT[c * 128:(c + 1) * 128, :]:
                       e.dma_start(out=d[:, 0:D], in_=s)),
              waits=[(Vqp, max(0, Vqp.n - 1)), (TbeC, max(0, TbeC.n - 1))],
              incs=[(qp, 16)])
            qp_t = qp.n
            for nb in range(2):
                E("tensor", (lambda e, stg=stg, c=c, nb=nb: e.matmul(
                    out=pQO[0:1, nb * 512:(nb + 1) * 512],
                    lhsT=bsbuf[:, c:c + 1], rhs=stg[:, nb * 512:(nb + 1) * 512],
                    start=(c == 0), stop=(c == 7))),
                  waits=([(qp, qp_t), (LdC, ldc_all)] if nb == 0 else []),
                  incs=([(TbeC, 1)] if nb == 1 else []))
            E("vector", (lambda e, stg=stg, c=c: e.tensor_scalar(
                out=wqkv_sb[c][:], in0=stg[:, 0:D], scalar1=gsbuf[:, c:c + 1],
                scalar2=None, op0=OP.mult)),
              waits=[(qp, qp_t), (TbeC, TbeC.n), (LdC, ldc_all)],
              incs=[(Vqp, 1)], reads=[f"wstage{c % 2}"], writes=[f"wqkv{c}"])
        E("vector", lambda e: e.tensor_copy(out=beff[:], in_=pQO[0:1, :]),
          waits=[(TbeC, 8)], incs=[(Vqp, 1)], writes=["beff"])
        for c in range(4):
            stg = wstage[c % 2]
            qp = LdQP[c % 2]
            E("sync", (lambda e, d=stg, s=woutT[c * 128:(c + 1) * 128, :]:
                       e.dma_start(out=d[:, 0:D], in_=s)),
              waits=[(Vqp, max(0, Vqp.n - 1))], incs=[(qp, 16)])
            E("vector", (lambda e, stg=stg, c=c: e.tensor_copy(
                out=wout_sb[c][:], in_=stg[:, 0:D])),
              waits=[(qp, qp.n)], incs=[(Vqp, 1)],
              reads=[f"wstage{c % 2}"], writes=[f"wout{c}"])
        vqp_prep = Vqp.n

        for (wt, nch, N, qbuf, col) in ((wgT, 8, DI, wqg_buf, 0),
                                        (wuT, 8, DI, wqu_buf, 1),
                                        (wdT, 32, D, wqd_buf, 2)):
            for c in range(nch):
                stg = wstage[c % 2]
                lw = LdW[c % 2]
                E("sync", (lambda e, d=stg, s=wt[c * 128:(c + 1) * 128, :], N=N:
                           e.dma_start(out=d[:, 0:N], in_=s)),
                  waits=[(Vw1, max(0, Vw1.n - 1)), (Vqp, vqp_prep)],
                  incs=[(lw, 16)])
                E("vector", (lambda e, stg=stg, c=c, N=N: e.tensor_reduce(
                    out=wsums[:, c:c + 1], in_=stg[:, 0:N], axis=AX.X, op=OP.add,
                    apply_absolute_value=True)),
                  waits=[(lw, lw.n)], incs=[(Vw1, 1)],
                  reads=[f"wstage{c % 2}", "wsums"], writes=["wsums"])
            E("tensor", (lambda e, nch=nch: e.matmul(
                out=pS[0][0:1, 0:nch], lhsT=onescol[:], rhs=wsums[:, 0:nch],
                start=True, stop=True)),
              waits=[(Vw1, Vw1.n), (LdC, ldc_all), (Vw2, Vw2.n)],
              incs=[(Tw, 1)])
            tw1 = Tw.n
            E("vector", (lambda e, nch=nch: e.tensor_reduce(
                out=wtot[0:1, 0:1], in_=pS[0][0:1, 0:nch], axis=AX.X, op=OP.add)),
              waits=[(Tw, tw1)], reads=["wtot"], writes=["wtot"])
            E("vector", (lambda e, nch=nch, N=N: e.tensor_scalar(
                out=wtot[0:1, 1:2], in0=wtot[0:1, 0:1],
                scalar1=1.0 / (nch * 128 * N), scalar2=1e-5,
                op0=OP.mult, op1=OP.max)),
              reads=["wtot"], writes=["wtot"])
            E("vector", lambda e: e.reciprocal(out=wtot[0:1, 2:3], in_=wtot[0:1, 1:2]),
              reads=["wtot"], writes=["wtot"], incs=[(Vw2, 1)])
            vw2a = Vw2.n
            E("tensor", (lambda e: e.matmul(
                out=pA[0][:, 0:2], lhsT=onesrow[:], rhs=wtot[0:1, 1:3],
                start=True, stop=True)),
              waits=[(Vw2, vw2a)], incs=[(Tw, 1)])
            tw2 = Tw.n
            E("vector", (lambda e, col=col: e.tensor_copy(
                out=invwbc[:, col:col + 1], in_=pA[0][:, 0:1])),
              waits=[(Tw, tw2)], writes=["invwbc"], incs=[(Vw2, 1)])
            E("vector", (lambda e: e.tensor_copy(out=rcp[:], in_=pA[0][:, 1:2])),
              reads=["rcp"], writes=["rcp"], incs=[(Vw2, 1)])
            vw2b = Vw2.n
            for c in range(nch):
                stg = wstage[c % 2]
                qstg = wqstage[0]
                lw = LdW[c % 2]
                sq = StWq[c % 2]
                E("sync", (lambda e, d=stg, s=wt[c * 128:(c + 1) * 128, :], N=N:
                           e.dma_start(out=d[:, 0:N], in_=s)),
                  waits=[(Vw1, max(0, Vw1.n - 1)), (Vw2, vw2b),
                         (Gw, max(0, Gw.n - 1))],
                  incs=[(lw, 16)])
                E("vector", (lambda e, stg=stg, N=N: e.tensor_scalar(
                    out=stg[:, 0:N], in0=stg[:, 0:N], scalar1=rcp[:, 0:1],
                    scalar2=MAGIC, op0=OP.mult, op1=OP.add)),
                  waits=[(lw, lw.n), (Vw2, vw2b)], incs=[(Vw1, 1)],
                  reads=[f"wstage{c % 2}", "rcp"], writes=[f"wstage{c % 2}"])
                E("vector", (lambda e, stg=stg, N=N: e.tensor_scalar(
                    out=stg[:, 0:N], in0=stg[:, 0:N], scalar1=MAGIC, scalar2=1.0,
                    op0=OP.subtract, op1=OP.min)),
                  reads=[f"wstage{c % 2}"], writes=[f"wstage{c % 2}"])
                E("vector", (lambda e, stg=stg, qstg=qstg, N=N: e.tensor_scalar(
                    out=qstg[:, 0:N], in0=stg[:, 0:N], scalar1=-1.0, scalar2=None,
                    op0=OP.max)),
                  waits=[(StWq[0], StWq[0].n), (StWq[1], StWq[1].n)],
                  incs=[(Gw, 1)], reads=[f"wstage{c % 2}"],
                  writes=["wqstage0"])
                E("sync", (lambda e, qstg=qstg, d=qbuf[c * 128:(c + 1) * 128, :],
                           N=N: e.dma_start(out=d, in_=qstg[:, 0:N])),
                  waits=[(Gw, Gw.n)], incs=[(sq, 16)])
        stwq_all = [(StWq[0], StWq[0].n), (StWq[1], StWq[1].n)]
        P.set_free(("pA", 0), (Vw2, Vw2.n))
        prolog_ps_free = [(Vw2, Vw2.n)]


        qrot_ready = {}
        krot_ready = {}
        vaug_ready = {}

        attn_state = {}

        def emit_attn_loop1(tp):
            par = tp % 2
            par4 = tp % 4
            w = tp % wps
            cs = [c for c in range(3) if 0 <= (w + c - 1) < wps]
            lo, hi = cs[0] * 128, (cs[-1] + 1) * 128
            sexp_at = {}
            attn_state[tp] = (cs, lo, hi, sexp_at)
            for h in range(NH):
                hp = h % 3
                bank = pS[hp]
                waits = [(Sexp, max(0, Sexp.n - 2))] + prolog_ps_free
                waits += qrot_ready[par4]
                for c in cs:
                    waits += krot_ready[(tp + c - 1) % 6]
                first = True
                last_is_mask2 = (2 in cs)
                for c in cs:
                    ks = (tp + c - 1) % 6
                    sim_last = (not last_is_mask2) and (c == cs[-1])
                    E("tensor", (lambda e, c=c, h=h, bank=bank, par4=par4,
                                 ks=ks: e.matmul(
                        out=bank[:, c * 128:(c + 1) * 128],
                        lhsT=qrot[par4][c][:, h * 128:(h + 1) * 128],
                        rhs=krot[ks][:, h * 128:(h + 1) * 128],
                        start=True, stop=(c == 1))),
                      waits=(waits if first else []),
                      incs=([(Tsim, 1)] if sim_last else []))
                    first = False
                    if c == 0:
                        E("tensor", (lambda e, bank=bank: e.matmul(
                            out=bank[:, 0:128], lhsT=ident[:],
                            rhs=tabs["mask0"][:], start=False, stop=True)))
                    if c == 2:
                        E("tensor", (lambda e, bank=bank: e.matmul(
                            out=bank[:, 256:384], lhsT=ident[:],
                            rhs=tabs["mask2"][:], start=False, stop=True)),
                          incs=[(Tsim, 1)])
                tsim_h = Tsim.n
                E("vector", (lambda e, bank=bank, hp=hp, lo=lo, hi=hi:
                             e.tensor_reduce(
                    out=nmax[hp][:], in_=bank[:, lo:hi], axis=AX.X, op=OP.max,
                    negate=True)),
                  waits=[(Tsim, tsim_h), (Sexp, max(0, Sexp.n - 2))],
                  incs=[(Vmax, 1)], reads=[f"nmax{hp}"], writes=[f"nmax{hp}"])
                vmax_h = Vmax.n
                wl = [(Vmax, vmax_h)] + exp_free.pop((par, h), [])
                E("scalar", (lambda e, bank=bank, h=h, hp=hp, lo=lo, hi=hi,
                             par=par: e.activation(
                    out=exp_sb[par][h][:, lo:hi], in_=bank[:, lo:hi],
                    func=AF.Exp, bias=nmax[hp][:], scale=1.0)),
                  waits=wl, incs=[(Sexp, 1)],
                  reads=[f"exp{par}_{h}"], writes=[f"exp{par}_{h}"])
                sexp_at[h] = Sexp.n

        def emit_attn_loop2(tp):
            par = tp % 2
            par4 = tp % 4
            cs, lo, hi, sexp_at = attn_state.pop(tp)
            vee_at = {}

            def emit_av(h):
                hp = h % 2
                bk2 = 0
                off = (h % 4) * 65
                for ci, c in enumerate(cs):
                    ks = (tp + c - 1) % 6
                    wl = []
                    if ci == 0:
                        wl = [(Vev, vee_at[h])] + vaug_ready[ks]
                        if h % 4 == 0:
                            wl += P.wait_free(("pA", bk2))
                    E("tensor", (lambda e, bk2=bk2, off=off, hp=hp, c=c, ks=ks,
                                 h=h, ci=ci, ncs=len(cs): e.matmul(
                        out=pA[bk2][:, off:off + 65],
                        lhsT=expT[hp][:, c * 128:(c + 1) * 128],
                        rhs=vaug[ks][:, h * 65:(h + 1) * 65],
                        start=(ci == 0), stop=(ci == ncs - 1))),
                      waits=wl,
                      incs=([(Tav, 1)] if ci == len(cs) - 1 else []))
                if h % 4 == 3:
                    tav_h = Tav.n
                    for hh in range(h - 3, h + 1):
                        off2 = (hh % 4) * 65
                        rb = rcps[hh % 4]
                        E("vector", (lambda e, bk2=bk2, off2=off2, rb=rb:
                                     e.reciprocal(
                            out=rb[:], in_=pA[bk2][:, off2 + 64:off2 + 65])),
                          waits=[(Tav, tav_h), (Sac, Sac.n)],
                          reads=[f"rcp{hh % 4}"],
                          writes=[f"rcp{hh % 4}"], incs=[(Vrc, 1)])
                    vrc_h = Vrc.n
                    for hh in range(h - 3, h + 1):
                        off2 = (hh % 4) * 65
                        rb = rcps[hh % 4]
                        E("scalar", (lambda e, bk2=bk2, off2=off2, hh=hh,
                                     par=par, rb=rb: e.activation(
                            out=attn_tm[par][:, hh * 64:(hh + 1) * 64],
                            in_=pA[bk2][:, off2:off2 + 64], func=AF.Copy,
                            scale=rb[:])),
                          waits=[(Vrc, vrc_h)], writes=[f"attntm{par}"],
                          incs=[(Sac, 1)])
                    P.set_free(("pA", bk2), (Sac, Sac.n))

            for h in range(NH):
                hp = h % 2
                wl = [(Sexp, sexp_at[h])] + P.wait_free(("pT", hp))
                for ci, c in enumerate(cs):
                    E("tensor", (lambda e, h=h, hp=hp, c=c, par=par: e.transpose(
                        out=pT[hp][:, c * 128:(c + 1) * 128],
                        in_=exp_sb[par][h][:, c * 128:(c + 1) * 128],
                        identity=ident[:])),
                      waits=(wl if ci == 0 else []),
                      incs=([(Tp, 1)] if ci == len(cs) - 1 else []))
                exp_free[(par, h)] = [(Tp, Tp.n)]
                wl = [(Tp, Tp.n), (Tav, max(0, Tav.n - 1))]
                E("scalar", (lambda e, hp=hp, lo=lo, hi=hi: e.copy(
                    out=expT[hp][:, lo:hi], in_=pT[hp][:, lo:hi])),
                  waits=wl, incs=[(Vev, 1)], writes=[f"expT{hp}"])
                P.set_free(("pT", hp), (Vev, Vev.n))
                vee_at[h] = Vev.n
                if h >= 1:
                    emit_av(h - 1)
            emit_av(NH - 1)
            vav_all = Sac.n
            tp_block(
                [attn_tm[par][:, c * 128:(c + 1) * 128] for c in range(4)],
                [(attnT[par][:], f"attnT{par}")], [0], Tp, Vev,
                extra_first_waits=[(Sac, vav_all)])
            veat = Vev.n
            wl0 = [(Vev, veat), (Vqk, Vqk.n), (Vpa, Vpa.n)]
            for nb in range(2):
                for k in range(4):
                    E("tensor", (lambda e, nb=nb, k=k, par=par: e.matmul(
                        out=pQO[:, nb * 512:(nb + 1) * 512],
                        lhsT=attnT[par][:, k * 128:(k + 1) * 128],
                        rhs=wout_sb[k][:, nb * 512:(nb + 1) * 512],
                        start=(k == 0), stop=(k == 3))),
                      waits=(wl0 if (nb == 0 and k == 0) else []),
                      incs=([(Tout, 1)] if (nb == 1 and k == 3) else []))
            tout_h = Tout.n
            E("vector", (lambda e, par=par, sl2=tp % 6: e.scalar_tensor_tensor(
                out=pa_sb[par][:], in0=pQO[:], scalar=1.0, in1=xb[sl2][:],
                op0=OP.mult, op1=OP.add)),
              waits=[(Tout, tout_h), (StPa[par], StPa[par].n)],
              incs=[(Vpa, 1)], writes=[f"pasb{par}"])
            E("gpsimd", (lambda e, par=par, tp=tp: e.dma_start(
                out=pa_buf[tp * 128:(tp + 1) * 128, :], in_=pa_sb[par][:])),
              waits=[(Vpa, Vpa.n)], incs=[(StPa[par], 16)])
            P.set_free(("xb", tp % 6), (Vpa, Vpa.n))
            for c in cs:
                P.set_free(("krot", (tp + c - 1) % 6), (Tsim, Tsim.n))
                P.set_free(("vaug", (tp + c - 1) % 6), (Tav, Tav.n))
            P.set_free(("qrot", par4), (Tsim, Tsim.n))
            P.set_free(("qrotg", par4), (Tsim, Tsim.n))

        exp_free = {}
        tqkv_at = {}

        def emit_tail(tt):
            par = tt % 2
            par4 = tt % 4
            sl = tt % 6
            wl = [(Tqkv, tqkv_at[tt])] + P.wait_free(("qk_sb", par))
            E("scalar", (lambda e, par=par: e.copy(
                out=qk_sb[par][:], in_=pQO[:, 0:512])),
              waits=wl, writes=[f"qksb{par}"])
            wl = P.wait_free(("vaug", sl)) + [(Vms, 6)]
            E("scalar", (lambda e, sl=sl: e.copy(
                out=vaug[sl][:].rearrange("p (h c) -> p h c", c=65)[:, :, 0:64],
                in_=pQO[:, 512:1024].rearrange("p (h c) -> p h c", c=64))),
              waits=wl, incs=[(Vqk, 1)], writes=[f"vaug{sl}"])
            vqk_t = Vqk.n
            vaug_ready[sl] = [(Vqk, vqk_t)]
            wl = [(Vqk, vqk_t)] + P.wait_free(("qkraw", par))
            for h in range(NH):
                bk2 = h // 4
                E("tensor", (lambda e, h=h, par=par, bk2=bk2: e.transpose(
                    out=pT[bk2][0:64, (h % 4) * 128:((h % 4) + 1) * 128],
                    in_=qk_sb[par][:, h * 64:(h + 1) * 64], identity=ident[:])),
                  waits=((wl + P.wait_free(("pT", 0))) if h == 0 else
                         (P.wait_free(("pT", 1)) if h == 4 else [])),
                  incs=([(Tp, 1)] if h % 4 == 3 else []))
            tpq = Tp.n
            wl2 = P.wait_free(("ev", f"qkraw{par}"))
            for bk2 in range(2):
                co = bk2 * 512
                E("scalar", (lambda e, bk2=bk2, par=par, co=co: e.copy(
                    out=qkraw[par][0:64, co:co + 512], in_=pT[bk2][0:64, :])),
                  waits=[(Tp, tpq - 1 + bk2)] + (wl2 if bk2 == 0 else []),
                  writes=[f"qkraw{par}"],
                  incs=([(Vev, 1)] if bk2 == 1 else []))
                E("scalar", (lambda e, bk2=bk2, par=par, co=co: e.copy(
                    out=qkraw[par][64:96, co:co + 512], in_=pT[bk2][32:64, :])),
                  waits=[(Tp, tpq - 1 + bk2)] + (wl2 if bk2 == 0 else []),
                  writes=[f"qkraw{par}"])
                E("scalar", (lambda e, bk2=bk2, par=par, co=co: e.copy(
                    out=qkraw[par][96:128, co:co + 512], in_=pT[bk2][0:32, :])),
                  writes=[f"qkraw{par}"],
                  incs=([(Vev, 1)] if bk2 == 1 else []))
            veqk_t = Vev.n
            P.set_free(("pT", 0), (Vev, veqk_t))
            P.set_free(("pT", 1), (Vev, veqk_t))
            P.set_free(("qk_sb", par), (Tp, tpq))

            for vi in range(3):
                wl0 = [(Vev, veqk_t)]
                if vi == 0:
                    wl0 += P.wait_free(("qrot", par4)) + P.wait_free(("qrotg", par4))
                E("vector", (lambda e, vi=vi, par=par, par4=par4: e.tensor_tensor(
                    out=qrot[par4][vi][:], in0=qkraw[par][:],
                    in1=tabs[f"rq{vi}"][:], op=OP.mult)),
                  waits=wl0, writes=[f"qrot{par4}_{vi}"],
                  incs=[(Vrot, 1)])
            qrot_ready[par4] = [(Vrot, Vrot.n)]
            wl0 = P.wait_free(("krot", sl))
            E("vector", (lambda e, par=par: e.tensor_tensor(
                out=rtv[0][:], in0=qkraw[par][0:64, :], in1=tabs["rkc"][:],
                op=OP.mult)),
              waits=wl0, reads=["rtv0"], writes=["rtv0"])
            E("vector", (lambda e, par=par: e.tensor_tensor(
                out=rtv[1][:], in0=qkraw[par][64:128, :],
                in1=tabs["rks"][64:128, :], op=OP.mult)),
              reads=["rtv1"], writes=["rtv1"])
            E("vector", (lambda e, sl=sl: e.tensor_tensor(
                out=krot[sl][0:64, :], in0=rtv[0][:], in1=rtv[1][:],
                op=OP.add)),
              reads=["rtv0", "rtv1"], writes=[f"krot{sl}"])
            E("vector", (lambda e, sl=sl: e.tensor_copy(
                out=krot[sl][64:128, :], in_=krot[sl][0:64, :])),
              reads=[f"krot{sl}"], writes=[f"krot{sl}"], incs=[(Vrot, 1)])
            krot_ready[sl] = [(Vrot, Vrot.n)]
            P.set_free(("qkraw", par), (Vrot, Vrot.n))


        for t in range(NT):
            par = t % 2
            par4 = t % 4
            sl = t % 6
            xsl = t % 6
            w = t % wps
            stt = stats[par]
            skey = f"stats{par}"
            if t >= 1:
                emit_tail(t - 1)
            wl = [x_loaded[t]] + P.wait_free(("stats", par))
            E("vector", (lambda e, stt=stt, sl=xsl: e.bn_stats(
                out=stt[:, 0:6], in_=xb[sl][:, 0:512])),
              waits=wl, reads=[skey + "mv"], writes=[skey + "bn"])
            E("vector", (lambda e, stt=stt, sl=xsl: e.bn_stats(
                out=stt[:, 6:12], in_=xb[sl][:, 512:1024])),
              writes=[skey + "bn"])
            E("vector", (lambda e, stt=stt: e.bn_aggr(
                out=stt[:, 12:14], in_=stt[:, 0:12])),
              reads=[skey + "bn"], writes=[skey + "mv"])
            E("vector", (lambda e, stt=stt: e.tensor_scalar(
                out=stt[:, 14:15], in0=stt[:, 13:14], scalar1=EPS, scalar2=None,
                op0=OP.add)), reads=[skey + "mv"], writes=[skey + "va"])
            E("vector", (lambda e, stt=stt: e.tensor_scalar(
                out=stt[:, 15:16], in0=stt[:, 13:14], scalar1=1.0 + EPS,
                scalar2=EPS * EPS, op0=OP.mult, op1=OP.add)),
              writes=[skey + "va"], incs=[(Vln, 1)])
            vln_t = Vln.n
            E("scalar", (lambda e, stt=stt: e.activation(
                out=stt[:, 16:18], in_=stt[:, 14:16], func=AF.Ln)),
              waits=[(Vln, vln_t)], writes=[skey + "s"])
            E("scalar", (lambda e, stt=stt: e.activation(
                out=stt[:, 18:20], in_=stt[:, 16:18], func=AF.Exp,
                scale=-0.5)),
              reads=[skey + "s"], writes=[skey + "s"], incs=[(Ssq, 1)])
            ssq_t = Ssq.n
            E("vector", (lambda e, stt=stt: e.tensor_tensor(
                out=stt[:, 22:24], in0=stt[:, 18:20], in1=stt[:, 18:20],
                op=OP.mult)),
              waits=[(Ssq, ssq_t)], reads=[skey + "tmp"], writes=[skey + "tmp"])
            E("vector", (lambda e, stt=stt: e.tensor_tensor(
                out=stt[:, 22:24], in0=stt[:, 22:24], in1=stt[:, 14:16],
                op=OP.mult)), reads=[skey + "tmp"], writes=[skey + "tmp"])
            E("vector", (lambda e, stt=stt: e.tensor_scalar(
                out=stt[:, 22:24], in0=stt[:, 22:24], scalar1=-0.5,
                scalar2=1.5, op0=OP.mult, op1=OP.add)),
              reads=[skey + "tmp"], writes=[skey + "tmp"])
            E("vector", (lambda e, stt=stt: e.tensor_tensor(
                out=stt[:, 20:22], in0=stt[:, 22:24], in1=stt[:, 18:20],
                op=OP.mult)),
              reads=[skey + "tmp"], writes=[skey + "r"], incs=[(Vst2, 1)])
            vst2_t = Vst2.n
            if t >= 3:
                emit_attn_loop1(t - 3)
            E("vector", (lambda e, stt=stt, sl=xsl, par=par: e.tensor_scalar(
                out=xn_sb[par][:], in0=xb[sl][:], scalar1=stt[:, 12:13],
                scalar2=stt[:, 20:21], op0=OP.subtract, op1=OP.mult)),
              waits=[(StXn[par], StXn[par].n)],
              reads=[skey + "r", skey + "mv"], incs=[(Sxn, 1)],
              writes=[f"xn{par}"])
            sxn_t = Sxn.n
            wlx2 = P.wait_free(("xn2b", par))
            E("vector", (lambda e, stt=stt, sl=xsl, par=par: e.tensor_scalar(
                out=xn2b[par][:], in0=xb[sl][:], scalar1=stt[:, 12:13],
                scalar2=stt[:, 21:22], op0=OP.subtract, op1=OP.mult)),
              waits=wlx2, incs=[(Sxn2, 1)], writes=[f"xn2b{par}"])
            sxn2_t = Sxn2.n
            P.set_free(("stats", par), (Sxn2, sxn2_t))
            E("gpsimd", (lambda e, par=par, t=t: e.dma_start(
                out=xn_buf[t * 128:(t + 1) * 128, :], in_=xn_sb[par][:])),
              waits=[(Sxn, sxn_t)], incs=[(StXn[par], 16)])
            if t >= 4:
                emit_attn_loop2(t - 4)
            wl = [(Sxn2, sxn2_t)] + P.wait_free(("xn2T", par))
            tp_block(
                [xn2b[par][:, c * 128:(c + 1) * 128] for c in range(8)],
                [(xn2T[par][:, 0:512], f"xn2T{par}a"),
                 (xn2T[par][:, 512:1024], f"xn2T{par}b")],
                [0, 1], Tp, Vev, extra_first_waits=wl)
            vex_t = Vev.n
            P.set_free(("xn2b", par), (Tp, Tp.n))
            wl = [(Vev, vex_t), (Vpa, Vpa.n), (Vqp, vqp_prep), (TbeC, 8),
                  (LdC, ldc_all)]
            for nb in range(2):
                for k in range(8):
                    E("tensor", (lambda e, nb=nb, k=k, par=par: e.matmul(
                        out=pQO[:, nb * 512:(nb + 1) * 512],
                        lhsT=xn2T[par][:, k * 128:(k + 1) * 128],
                        rhs=wqkv_sb[k][:, nb * 512:(nb + 1) * 512],
                        start=(k == 0), stop=False)),
                      waits=(wl if (nb == 0 and k == 0) else []))
                E("tensor", (lambda e, nb=nb: e.matmul(
                    out=pQO[:, nb * 512:(nb + 1) * 512], lhsT=onesrowb[:],
                    rhs=beff[0:1, nb * 512:(nb + 1) * 512],
                    start=False, stop=True)),
                  incs=([(Tqkv, 1)] if nb == 1 else []))
            tqkv_at[t] = Tqkv.n
            P.set_free(("xn2T", par), (Tqkv, Tqkv.n))
            if t + 2 < NT:
                wl = P.wait_free(("xb", (t + 2) % 6))
                sl2 = (t + 2) % 6
                E("sync", (lambda e, t2=t + 2: e.dma_start(
                    out=xb[t2 % 6][:], in_=x_in[t2 * 128:(t2 + 1) * 128, :])),
                  waits=wl, incs=[(LdX[sl2], 16)])
                x_loaded[t + 2] = (LdX[sl2], LdX[sl2].n)

        emit_tail(NT - 1)
        if NT >= 4:
            emit_attn_loop2(NT - 4)
        for tp in range(max(0, NT - 3), NT):
            emit_attn_loop1(tp)
            emit_attn_loop2(tp)

        E("sync", lambda e: e.nop(),
          waits=[(StXn[0], StXn[0].n), (StXn[1], StXn[1].n),
                 (StPa[0], StPa[0].n), (StPa[1], StPa[1].n)] + stwq_all)
        P.barrier("barAB")

        ast.close()
        bstk = ExitStack()
        sb = lambda name, shape, dt: bstk.enter_context(nc.sbuf_tensor(name, shape, dt))
        ps = lambda name, shape, dt: bstk.enter_context(nc.psum_tensor(name, shape, dt))
        wgq = [sb(f"wgq{i}", [128, DI], BF16) for i in range(8)]
        wuq = [sb(f"wuq{i}", [128, DI], BF16) for i in range(8)]
        xn_in = [sb(f"xnin{i}", [128, D], F32) for i in range(2)]
        xq_b = [sb(f"xqb{i}", [128, D], BF16) for i in range(2)]
        xqT = [sb(f"xqT{i}", [128, D], BF16) for i in range(2)]
        g_sb = [sb(f"gsb{i}", [128, 512], F32) for i in range(2)]
        h_sb = sb("h_sb", [128, DI], F32)
        hr_sb = sb("hr_sb", [128, DI], F32)
        hq_sb = sb("hq_sb", [128, DI], BF16)
        bst = [sb(f"bstA{i}", [128, 16], F32) for i in range(2)]

        pG = [ps(f"pG{i}", [128, 512], F32) for i in range(2)]
        pU = [ps(f"pU{i}", [128, 512], F32) for i in range(2)]

        LdWg = S("LdWg")
        LdXn = [S(f"LdXn{i}") for i in range(2)]
        Vam = S("Vam"); Vxq = S("Vxq")
        Tg = S("Tg"); Tu = S("Tu"); Ssil = S("Ssil"); Vh = S("Vh")
        Vsh = S("Vsh"); Gq = S("Gq"); StHq = S("StHq")

        for k in range(8):
            E("sync", (lambda e, k=k: e.dma_start(
                out=wgq[k][:], in_=wqg_buf[k * 128:(k + 1) * 128, :])),
              incs=[(LdWg, 16)])
            E("sync", (lambda e, k=k: e.dma_start(
                out=wuq[k][:], in_=wqu_buf[k * 128:(k + 1) * 128, :])),
              incs=[(LdWg, 16)])
        ldwg_all = LdWg.n
        xn_loaded = {}
        E("sync", (lambda e: e.dma_start(
            out=xn_in[0][:], in_=xn_buf[0:128, :])), incs=[(LdXn[0], 16)])
        xn_loaded[0] = (LdXn[0], 16)

        vexq_at = {}
        vam_at = {}
        vxq_at = {}

        def emit_xq(tt):
            par = tt % 2
            stt = bst[par]
            skey = f"bstA{par}"
            wl = [xn_loaded[tt]] + P.wait_free(("bst", par))
            E("vector", (lambda e, stt=stt, par=par: e.tensor_reduce(
                out=stt[:, 0:1], in_=xn_in[par][:], axis=AX.X, op=OP.max,
                apply_absolute_value=True)),
              waits=wl, reads=[skey], writes=[skey])
            E("vector", (lambda e, stt=stt: e.tensor_scalar(
                out=stt[:, 1:2], in0=stt[:, 0:1], scalar1=1e-5,
                scalar2=1.0 / 127.0, op0=OP.max, op1=OP.mult)),
              reads=[skey], writes=[skey])
            E("vector", (lambda e, stt=stt: e.reciprocal(
                out=stt[:, 2:3], in_=stt[:, 1:2])),
              reads=[skey], writes=[skey])
            E("vector", (lambda e, stt=stt: e.tensor_scalar(
                out=stt[:, 3:4], in0=invwbc[:, 0:1], scalar1=stt[:, 1:2],
                scalar2=None, op0=OP.mult)), reads=[skey], writes=[skey])
            E("vector", (lambda e, stt=stt: e.tensor_scalar(
                out=stt[:, 4:5], in0=invwbc[:, 1:2], scalar1=stt[:, 1:2],
                scalar2=None, op0=OP.mult)),
              reads=[skey], writes=[skey], incs=[(Vam, 1)])
            vam_at[tt] = Vam.n
            E("vector", (lambda e, stt=stt, par=par: e.tensor_scalar(
                out=xn_in[par][:], in0=xn_in[par][:], scalar1=stt[:, 2:3],
                scalar2=MAGIC, op0=OP.mult, op1=OP.add)),
              reads=[f"xnin{par}", skey], writes=[f"xnin{par}"])
            E("vector", (lambda e, par=par: e.tensor_scalar(
                out=xn_in[par][:], in0=xn_in[par][:], scalar1=MAGIC,
                scalar2=127.0, op0=OP.subtract, op1=OP.min)),
              reads=[f"xnin{par}"], writes=[f"xnin{par}"])
            wl = P.wait_free(("xq_b", par))
            E("vector", (lambda e, par=par: e.tensor_scalar(
                out=xq_b[par][:], in0=xn_in[par][:], scalar1=-128.0,
                scalar2=None, op0=OP.max)),
              waits=wl, reads=[f"xnin{par}"], writes=[f"xqb{par}"],
              incs=[(Vxq, 1)])
            vxq_t = Vxq.n
            P.set_free(("xn_in", par), (Vxq, vxq_t))
            vxq_at[tt] = vxq_t

        def emit_xqT(tt):
            par = tt % 2
            wl = [(Vxq, vxq_at[tt]), (LdWg, ldwg_all)] \
                + P.wait_free(("xqT", par))
            tp_block(
                [xq_b[par][:, c * 128:(c + 1) * 128] for c in range(8)],
                [(xqT[par][:, 0:512], f"xqT{par}a"),
                 (xqT[par][:, 512:1024], f"xqT{par}b")],
                [0, 1], Tp, Vev, extra_first_waits=wl)
            vexq_at[tt] = Vev.n
            P.set_free(("xq_b", par), (Tp, Tp.n))

        emit_xq(0)
        emit_xqT(0)

        def emit_rounds(tt):
            E("vector", (lambda e: e.tensor_scalar(
                out=hr_sb[:], in0=hr_sb[:], scalar1=MAGIC, scalar2=127.0,
                op0=OP.subtract, op1=OP.min)),
              reads=["hr_sb"], writes=["hr_sb"])
            E("vector", (lambda e: e.tensor_scalar(
                out=hq_sb[:], in0=hr_sb[:], scalar1=-128.0, scalar2=None,
                op0=OP.max)),
              waits=[(StHq, StHq.n)],
              reads=["hr_sb"], writes=["hq_sb"], incs=[(Gq, 1)])
            E("gpsimd", (lambda e, tt=tt: e.dma_start(
                out=hq_buf[tt * 128:(tt + 1) * 128, :], in_=hq_sb[:])),
              waits=[(Gq, Gq.n)], incs=[(StHq, 16)])

        for t in range(NT):
            par = t % 2
            stt = bst[par]
            skey = f"bstA{par}"
            if t + 1 < NT:
                pr2 = (t + 1) % 2
                wl = P.wait_free(("xn_in", pr2))
                E("sync", (lambda e, t2=t + 1: e.dma_start(
                    out=xn_in[t2 % 2][:],
                    in_=xn_buf[t2 * 128:(t2 + 1) * 128, :])),
                  waits=wl, incs=[(LdXn[pr2], 16)])
                xn_loaded[t + 1] = (LdXn[pr2], LdXn[pr2].n)
                emit_xq(t + 1)
            vexq_t = vexq_at[t]
            vam_t = vam_at[t]
            for n in range(8):
                np_ = n % 2
                wlg = [(Vev, vexq_t), (Ssil, max(0, Ssil.n - 1))]
                for k in range(8):
                    E("tensor", (lambda e, n=n, k=k, par=par, np_=np_: e.matmul(
                        out=pG[np_][:], lhsT=xqT[par][:, k * 128:(k + 1) * 128],
                        rhs=wgq[k][:, n * 512:(n + 1) * 512],
                        start=(k == 0), stop=(k == 7))),
                      waits=(wlg if k == 0 else []),
                      incs=([(Tg, 1)] if k == 7 else []))
                tg_n = Tg.n
                wlu = [(Vh, max(0, Vh.n - 1))]
                for k in range(8):
                    E("tensor", (lambda e, n=n, k=k, par=par, np_=np_: e.matmul(
                        out=pU[np_][:], lhsT=xqT[par][:, k * 128:(k + 1) * 128],
                        rhs=wuq[k][:, n * 512:(n + 1) * 512],
                        start=(k == 0), stop=(k == 7))),
                      waits=(wlu if k == 0 else []),
                      incs=([(Tu, 1)] if k == 7 else []))
                tu_n = Tu.n
                E("scalar", (lambda e, np_=np_, stt=stt: e.activation(
                    out=g_sb[np_][:], in_=pG[np_][:], func=AF.Silu,
                    scale=stt[:, 3:4])),
                  waits=[(Tg, tg_n), (Vam, vam_t), (Vh, max(0, Vh.n - 1))],
                  incs=[(Ssil, 1)], writes=[f"gsb{np_}"])
                ssil_n = Ssil.n
                wlh = [(Tu, tu_n), (Ssil, ssil_n)]
                if n == 0:
                    wlh += P.wait_free(("h_sb",))
                E("vector", (lambda e, n=n, np_=np_, stt=stt:
                             e.scalar_tensor_tensor(
                    out=h_sb[:, n * 512:(n + 1) * 512], in0=pU[np_][:],
                    scalar=stt[:, 4:5], in1=g_sb[np_][:],
                    op0=OP.mult, op1=OP.mult)),
                  waits=wlh, reads=["h_sb"], writes=["h_sb"], incs=[(Vh, 1)])
                E("vector", (lambda e, n=n, stt=stt: e.tensor_reduce(
                    out=stt[:, 8 + n:9 + n],
                    in_=h_sb[:, n * 512:(n + 1) * 512],
                    axis=AX.X, op=OP.max, apply_absolute_value=True)),
                  reads=["h_sb"], writes=[skey])
            if t + 1 < NT:
                emit_xqT(t + 1)
            if t >= 1:
                emit_rounds(t - 1)
            E("vector", (lambda e, stt=stt: e.tensor_reduce(
                out=stt[:, 5:6], in_=stt[:, 8:16], axis=AX.X, op=OP.max)),
              reads=[skey], writes=[skey])
            E("vector", (lambda e, stt=stt: e.tensor_scalar(
                out=stt[:, 6:7], in0=stt[:, 5:6], scalar1=1e-5,
                scalar2=1.0 / 127.0, op0=OP.max, op1=OP.mult)),
              reads=[skey], writes=[skey])
            E("vector", (lambda e, stt=stt: e.reciprocal(
                out=stt[:, 7:8], in_=stt[:, 6:7])),
              reads=[skey], writes=[skey])
            E("vector", (lambda e, stt=stt, t=t: e.tensor_scalar(
                out=combs[:, t:t + 1], in0=invwbc[:, 2:3], scalar1=stt[:, 6:7],
                scalar2=None, op0=OP.mult)),
              reads=[skey], writes=["combs"], incs=[(Vsh, 1)])
            vsh_t = Vsh.n
            P.set_free(("bst", par), (Vsh, vsh_t))
            E("vector", (lambda e, stt=stt: e.tensor_scalar(
                out=hr_sb[:], in0=h_sb[:], scalar1=stt[:, 7:8], scalar2=MAGIC,
                op0=OP.mult, op1=OP.add)),
              waits=[(Vsh, vsh_t)], reads=["h_sb", "hr_sb"], writes=["hr_sb"],
              incs=[(Vh, 1)])
            P.set_free(("h_sb",), (Vh, Vh.n))

        emit_rounds(NT - 1)
        E("sync", lambda e: e.nop(), waits=[(StHq, 16 * NT)])
        P.barrier("barBC")

        bstk.close()
        cstk = ExitStack()
        sb = lambda name, shape, dt: cstk.enter_context(nc.sbuf_tensor(name, shape, dt))
        ps = lambda name, shape, dt: cstk.enter_context(nc.psum_tensor(name, shape, dt))
        wdq = [sb(f"wdq{i}", [128, D], BF16) for i in range(32)]
        hq_in = [sb(f"hqin{i}", [128, DI], BF16) for i in range(2)]
        hqT = [sb(f"hqTt{i}", [128, DI], BF16) for i in range(2)]
        pa_in = [sb(f"pain{i}", [128, D], F32) for i in range(2)]
        y_sb = [sb(f"ysb{i}", [128, D], F32) for i in range(2)]

        pD = [ps(f"pD{i}", [128, 1024], F32) for i in range(2)]

        LdWd = S("LdWd")
        LdHq = [S(f"LdHq{i}") for i in range(2)]
        LdPa = [S(f"LdPa{i}") for i in range(2)]
        Td = S("Td"); Vy = S("Vy")
        StY = [S(f"StY{i}") for i in range(2)]

        for k in range(32):
            E("sync", (lambda e, k=k: e.dma_start(
                out=wdq[k][:], in_=wqd_buf[k * 128:(k + 1) * 128, :])),
              incs=[(LdWd, 16)])
        ldwd_all = LdWd.n
        hq_loaded = {}
        pa_loaded = {}
        E("sync", (lambda e: e.dma_start(
            out=hq_in[0][:], in_=hq_buf[0:128, :])), incs=[(LdHq[0], 16)])
        hq_loaded[0] = (LdHq[0], 16)
        E("sync", (lambda e: e.dma_start(
            out=pa_in[0][:], in_=pa_buf[0:128, :])), incs=[(LdPa[0], 16)])
        pa_loaded[0] = (LdPa[0], 16)

        vehq_at = {}

        def emit_hqT(tt):
            par = tt % 2
            wl = [hq_loaded[tt], (LdWd, ldwd_all)] + P.wait_free(("hqT", par))
            tp_block(
                [hq_in[par][:, c * 128:(c + 1) * 128] for c in range(32)],
                [(hqT[par][:, b * 512:(b + 1) * 512], f"hqT{par}_{b}")
                 for b in range(8)],
                [0, 1, 0, 1, 0, 1, 0, 1], Tp, Vev, extra_first_waits=wl,
                evac_eng="vector")
            vehq_at[tt] = Vev.n
            P.set_free(("hq_in", par), (Tp, Tp.n))

        emit_hqT(0)

        for t in range(NT):
            par = t % 2
            if t + 1 < NT:
                pr2 = (t + 1) % 2
                wl = P.wait_free(("hq_in", pr2))
                E("sync", (lambda e, t2=t + 1: e.dma_start(
                    out=hq_in[t2 % 2][:],
                    in_=hq_buf[t2 * 128:(t2 + 1) * 128, :])),
                  waits=wl, incs=[(LdHq[pr2], 16)])
                hq_loaded[t + 1] = (LdHq[pr2], LdHq[pr2].n)
                wl = P.wait_free(("pa_in", pr2))
                E("sync", (lambda e, t2=t + 1: e.dma_start(
                    out=pa_in[t2 % 2][:],
                    in_=pa_buf[t2 * 128:(t2 + 1) * 128, :])),
                  waits=wl, incs=[(LdPa[pr2], 16)])
                pa_loaded[t + 1] = (LdPa[pr2], LdPa[pr2].n)
                emit_hqT(t + 1)
            wl = [(Vev, vehq_at[t])] + P.wait_free(("pD", par))
            for nb in range(2):
                for k in range(32):
                    E("tensor", (lambda e, nb=nb, k=k, par=par: e.matmul(
                        out=pD[par][:, nb * 512:(nb + 1) * 512],
                        lhsT=hqT[par][:, k * 128:(k + 1) * 128],
                        rhs=wdq[k][:, nb * 512:(nb + 1) * 512],
                        start=(k == 0), stop=(k == 31))),
                      waits=(wl if (nb == 0 and k == 0) else []),
                      incs=([(Td, 1)] if (nb == 1 and k == 31) else []))
            td_t = Td.n
            P.set_free(("hqT", par), (Td, td_t))
            E("vector", (lambda e, par=par, t=t: e.scalar_tensor_tensor(
                out=y_sb[par][:], in0=pD[par][:], scalar=combs[:, t:t + 1],
                in1=pa_in[par][:], op0=OP.mult, op1=OP.add)),
              waits=[(Td, td_t), pa_loaded[t],
                     (StY[par], StY[par].n)],
              incs=[(Vy, 1)], writes=[f"ysb{par}"])
            vy_t = Vy.n
            P.set_free(("pD", par), (Vy, vy_t))
            P.set_free(("pa_in", par), (Vy, vy_t))
            E("gpsimd", (lambda e, par=par, t=t: e.dma_start(
                out=y_out[t * 128:(t + 1) * 128, :], in_=y_sb[par][:])),
              waits=[(Vy, vy_t)], incs=[(StY[par], 16)])

        E("sync", lambda e: e.nop(),
          waits=[(StY[0], StY[0].n), (StY[1], StY[1].n)])
        cstk.close()

        with nc.Block() as blk:
            P.replay(blk)
    return nc


_NC_CACHE = {}


def kernel(L0_attn_x, L0_attn_ln_g, L0_attn_ln_b, L0_attn_w_qkv, L0_attn_w_out,
           L0_ffn_w_gate, L0_ffn_w_up, L0_ffn_w_down):
    n_seq, wps = 2, 32
    x = np.ascontiguousarray(np.asarray(L0_attn_x, dtype=np.float32))
    B, Sq, Dm = x.shape
    cores = 8
    key = (n_seq, wps)
    if key not in _NC_CACHE:
        _NC_CACHE[key] = build_nc(n_seq, wps)
    nc = _NC_CACHE[key]

    consts = host_consts()
    shared = {
        "wqkvT": np.ascontiguousarray(np.asarray(L0_attn_w_qkv, np.float32).T),
        "woutT": np.ascontiguousarray(np.asarray(L0_attn_w_out, np.float32).T),
        "wgT": np.ascontiguousarray(np.asarray(L0_ffn_w_gate, np.float32).T),
        "wuT": np.ascontiguousarray(np.asarray(L0_ffn_w_up, np.float32).T),
        "wdT": np.ascontiguousarray(np.asarray(L0_ffn_w_down, np.float32).T),
        "g_pc": np.ascontiguousarray(
            np.asarray(L0_attn_ln_g, np.float32).reshape(8, 128).T),
        "b_pc": np.ascontiguousarray(
            np.asarray(L0_attn_ln_b, np.float32).reshape(8, 128).T),
    }
    for k, v in consts.items():
        shared[k] = np.ascontiguousarray(v)

    in_maps = []
    for c in range(cores):
        m = dict(shared)
        m["x"] = np.ascontiguousarray(
            x[c * n_seq:(c + 1) * n_seq].reshape(n_seq * Sq, Dm))
        in_maps.append(m)

    res = run_bass_kernel_spmd(nc, in_maps, core_ids=list(range(cores)))
    out = np.concatenate(
        [np.asarray(r["y"], np.float32).reshape(n_seq, Sq, Dm)
         for r in res.results], axis=0)
    return out
